# version 19
# baseline (speedup 1.0000x reference)
import numpy as np
import concourse.bass as bass
import concourse.mybir as mybir
from concourse.bass_utils import run_bass_kernel_spmd

F32 = mybir.dt.float32
BF16 = mybir.dt.bfloat16
AF = mybir.ActivationFunctionType
ALU = mybir.AluOpType
AXX = mybir.AxisListType.X

B, T, Dm = 16, 2048, 1024
NCORES = 8
SEQ = B // NCORES
TPC = SEQ * T
TT = 512
NT = TPC // TT
TPS = NT // SEQ
NSUB = TT // 128
KD = Dm // 128
NSUBS = NT * NSUB
EPS = 1e-5


class _DummyInstr:
    def then_inc(self, *a, **k):
        return self


class _DummyEng:
    def __getattr__(self, name):
        def f(*a, **kw):
            return _DummyInstr()
        return f


class Ctx:
    SEMS = ["sW1", "sW2", "sW3", "sGs", "sX0", "sX1", "sLN0", "sLN1",
            "sLB0", "sLB1", "sYt0", "sYt1", "sYs0", "sYs1", "sOUT",
            "sT", "sV", "sS", "sVt", "sTS0", "sTS1", "sXA0", "sXA1"]

    def __init__(self):
        self.emit = False
        self.handles = {}
        self.cnt = {s: 0 for s in self.SEMS}
        self.marks = {}

    def reset(self, emit):
        self.emit = emit
        self.cnt = {s: 0 for s in self.SEMS}

    def inc(self, instr, sem, n=1):
        self.cnt[sem] += n
        if self.emit:
            instr.then_inc(self.handles[sem], n)
        return instr

    def mark(self, key, sem):
        self.marks[key] = (sem, self.cnt[sem])

    def wait(self, eng, key):
        sem, val = self.marks.get(key, (None, 0))
        if self.emit and val > 0:
            eng.wait_ge(self.handles[sem], val)

    def chain(self, eng, instr):
        self.inc(instr, "sVt", 1)
        if self.emit:
            eng.wait_ge(self.handles["sVt"], self.cnt["sVt"])

    def epoch(self, eng, sem):
        if self.emit and self.cnt[sem] > 0:
            eng.wait_ge(self.handles[sem], self.cnt[sem])


def build_program():
    nc = bass.Bass("TRN2", target_bir_lowering=False)

    u = nc.dram_tensor("u", [SEQ, T, Dm], F32, kind="ExternalInput").ap()
    DI = lambda n, s: nc.dram_tensor(n, list(s), F32, kind="ExternalInput").ap()
    waT_d = DI("waT", (Dm, Dm))
    wbT_d = DI("wbT", (Dm, Dm))
    cT_d = DI("cT", (Dm, Dm))
    w1_d = DI("w1", (Dm, Dm))
    w2_d = DI("w2", (Dm, 512))
    w3_d = DI("w3", (512, 256))
    w4_d = DI("w4", (256, 1))
    g1_d = DI("g1", (Dm, 512))
    g2_d = DI("g2", (512, 256))
    g3_d = DI("g3", (256, 1))
    ba_d = DI("ba", (128, 8))
    bbrow_d = DI("bbrow_d", (1, Dm))
    b1_d = DI("b1", (128, 8))
    b2_d = DI("b2", (128, 4))
    b3_d = DI("b3", (128, 2))
    b4_d = DI("b4", (1, 1))
    gb1_d = DI("gb1", (128, 4))
    gb2_d = DI("gb2", (128, 2))
    gb3_d = DI("gb3", (1, 1))
    ar_d = DI("ar", (1, 1))
    dstr_d = DI("dstr", (8, 128, 512))
    out = nc.dram_tensor("out", [SEQ, T, Dm], F32, kind="ExternalOutput").ap()

    xln_scr = nc.dram_tensor("xln_scr", [TPC, Dm], BF16, kind="Internal").ap()
    x1_scr = nc.dram_tensor("x1_scr", [TPC, Dm], F32, kind="Internal").ap()
    z_scr = nc.dram_tensor("z_scr", [TPC, Dm], BF16, kind="Internal").ap()

    A = lambda name, shape, dt, **kw: nc.alloc_sbuf_tensor(name, shape, dt, **kw).ap()

    WB = A("WB", [128, 27008], BF16)
    waT = [WB[:, k * 1024:(k + 1) * 1024] for k in range(KD)]
    wbT = [WB[:, 8192 + k * 1024:8192 + (k + 1) * 1024] for k in range(KD)]
    cT = [WB[:, 16384 + k * 1024:16384 + (k + 1) * 1024] for k in range(KD)]
    w1 = [WB[:, k * 1024:(k + 1) * 1024] for k in range(KD)]
    w1p = [WB[:, 8192 + k * 1024:8192 + (k + 1) * 1024] for k in range(KD)]
    w2 = [WB[:, 16384 + k * 512:16384 + (k + 1) * 512] for k in range(KD)]
    w3 = [WB[:, 20480 + k * 256:20480 + (k + 1) * 256] for k in range(4)]
    w4 = [WB[:, 21504 + k:21504 + k + 1] for k in range(2)]


    dstrb = A("dstrb", [128, KD * 512], BF16)
    ba2 = A("ba2", [128, 8], F32)
    bac = A("bac", [128, 8], F32)
    b1c = A("b1c", [128, 8], F32)
    b2c = A("b2c", [128, 4], F32)
    b3c = A("b3c", [128, 2], F32)
    b4h = A("b4h", [1, 2], F32)
    gb1c = A("gb1c", [128, 4], F32)
    gb2c = A("gb2c", [128, 2], F32)
    gb3h = A("gb3h", [1, 2], F32)
    arv = A("arv", [1, 1], F32)
    bbrow = A("bbrow", [1, Dm], BF16)
    ones_f = A("ones_f", [1, 128], F32)
    ones_bf = A("ones_bf", [1, 128], BF16)
    ones_b1 = A("ones_b1", [1, TT], BF16)
    onesc_b = A("onesc_b", [128, 1], BF16)
    id1f = A("id1f", [1, 1], F32)
    id1b = A("id1b", [1, 1], BF16)

    bnraw = A("bnraw", [128, NSUBS * 12], F32)
    mean1 = A("mean1", [128, NSUBS], F32)
    r1 = A("r1", [128, NSUBS], F32)
    mean2 = A("mean2", [128, NSUBS], F32)
    r2t = A("r2t", [128, NSUBS], F32)
    svar = A("svar", [128, NSUBS], F32)
    ssq = A("ssq", [128, NSUBS], F32)
    st1 = A("st1", [128, NSUBS], F32)
    st2 = A("st2", [128, NSUBS], F32)
    stsq = A("stsq", [128, NSUBS * 4], F32)
    carry = A("carry", [128, 8], F32)

    xb = [A("xb0", [128, NSUB * Dm], F32), A("xb1", [128, NSUB * Dm], F32)]
    xlnbf = [A("xlnbf0", [128, NSUB * Dm], BF16), A("xlnbf1", [128, NSUB * Dm], BF16)]
    xfm = [A("xfm0", [128, KD * TT], BF16, is_transpose_dest=True),
           A("xfm1", [128, KD * TT], BF16, is_transpose_dest=True)]
    alf = A("alf", [128, KD * TT], F32)
    hsf = A("hsf", [128, KD * TT], F32)
    hsbf = [A("hsbf0", [128, KD * TT], BF16), A("hsbf1", [128, KD * TT], BF16)]
    bex = A("bex", [128, 4096], F32)
    bex2 = A("bex2", [128, 640], F32)
    fsc = A("fsc", [128, 512], F32)
    cvf = A("cvf", [128, 5], F32)

    x1b = xb
    zbf = xlnbf
    zfm = xfm
    alf_b = alf.bitcast(BF16)
    hsf_b = hsf.bitcast(BF16)
    zsl = [hsf_b[:, 0:4096], hsf_b[:, 4096:8192]]
    G0 = [alf_b[:, 0:4096], hsf_b[:, 0:4096]]
    H0 = [alf_b[:, 4096:8192], hsf_b[:, 4096:8192]]
    Gcur = [xlnbf[0][:, 0:4096], hsbf[0][:, 0:4096]]
    h1t = [xlnbf[1][:, 0:4096], hsbf[1][:, 0:4096]]
    bex_b = bex.bitcast(BF16)
    g1 = [bex_b[:, k * 512:(k + 1) * 512] for k in range(KD)]
    g2 = [bex_b[:, 4096 + k * 256:4096 + (k + 1) * 256] for k in range(4)]
    g3 = [bex_b[:, 5120 + k:5120 + k + 1] for k in range(2)]
    h2t = [bex_b[:, 0:2048], bex_b[:, 2048:4096]]
    h3t = [bex_b[:, 4096:5120], bex_b[:, 5120:6144]]
    C1b = [bex_b[:, 6144:6656], bex_b[:, 6656:7168]]
    P2b = [bex_b[:, 7168:7680], bex_b[:, 7680:8192]]
    bex2_b = bex2.bitcast(BF16)
    Q2b = [bex2_b[:, 0:512], bex2_b[:, 512:1024]]
    gfmb = bex2_b[:, 1024:1040]
    hg1 = bex2_b[:, 1040:1048]
    hg2 = bex2_b[:, 1048:1052]
    ctv = [A("ctv0", [128, 12], F32), A("ctv1", [128, 12], F32)]
    cco = [A("cco0", [128, 24], F32), A("cco1", [128, 24], F32)]
    cf_ = A("cf0", [1, 2560], BF16)
    cf = [cf_, cf_]
    t4b_ = A("t4b", [1, TT], F32)
    t4b = [t4b_, t4b_]
    gsb = cf_[0:1, 0:2 * Dm]
    tg = st2[0:1, 0:2]
    A2 = st2[0:1, 2:4]
    Bc = st2[0:1, 4:5]

    pb = [nc.alloc_psum_tensor("pb%d" % i, [128, 512], F32).ap() for i in range(8)]
    pbb2 = pb[2].bitcast(BF16)
    pbb6 = pb[6].bitcast(BF16)

    assert nc.sbuf_bytes_remaining >= 0

    ctx = Ctx()
    _li = [_DummyInstr()]

    def seq_of(i):
        return i // TPS

    def t0_of(i):
        return (i % TPS) * TT

    def tok0(i):
        return i * TT

    def rearr_d(d2):
        return d2.rearrange("(j p) d -> p j d", p=128)

    def emit_gpsimd(eng):
        for dst, src in [(bac, ba_d), (b1c, b1_d), (b2c, b2_d), (b3c, b3_d),
                         (gb1c, gb1_d), (gb2c, gb2_d)]:
            ctx.inc(eng.dma_start(out=dst[:, :], in_=src), "sGs", 16)
        ctx.inc(eng.dma_start(out=b4h[0:1, 0:1], in_=b4_d), "sGs", 16)
        ctx.inc(eng.dma_start(out=gb3h[0:1, 0:1], in_=gb3_d), "sGs", 16)
        ctx.inc(eng.dma_start(out=arv[:, :], in_=ar_d), "sGs", 16)
        ctx.inc(eng.dma_start(out=bbrow[:, :], in_=bbrow_d), "sGs", 16)
        ctx.mark(("smalls",), "sGs")

        def plxload(i):
            sx = "sX%d" % (i % 2)
            if i >= 1:
                ctx.wait(eng, ("vln", i - 1))
            ctx.epoch(eng, sx)
            ctx.inc(eng.dma_start(
                out=bex.rearrange("p (j d) -> p j d", d=Dm),
                in_=rearr_d(u[seq_of(i), t0_of(i):t0_of(i) + TT, :])), sx, 16)
            ctx.mark(("plx", i), sx)

        plxload(0)
        for k in range(KD):
            ctx.inc(eng.dma_start(out=waT[k], in_=waT_d[k * 128:(k + 1) * 128, :]), "sW1", 16)
        ctx.mark(("wattn1",), "sW1")
        ctx.epoch(eng, "sGs")
        ctx.inc(eng.dma_start(out=dstrb.rearrange("p (q c) -> p q c", c=512),
                              in_=dstr_d.rearrange("q p c -> p q c")), "sGs", 16)
        ctx.mark(("dstrl",), "sGs")
        plxload(1)
        for k in range(KD):
            ctx.inc(eng.dma_start(out=wbT[k], in_=wbT_d[k * 128:(k + 1) * 128, :]), "sW2", 16)
        ctx.mark(("wattn2",), "sW2")
        plxload(2)
        for k in range(KD):
            ctx.inc(eng.dma_start(out=cT[k], in_=cT_d[k * 128:(k + 1) * 128, :]), "sW3", 16)
        ctx.mark(("wattn3",), "sW3")
        for i in range(3, NT):
            plxload(i)
        ctx.wait(eng, ("vln", NT - 1))
        ctx.epoch(eng, "sW3")
        for k in range(KD):
            ctx.inc(eng.dma_start(out=g1[k], in_=g1_d[k * 128:(k + 1) * 128, :]), "sW3", 16)
        for k in range(4):
            ctx.inc(eng.dma_start(out=g2[k], in_=g2_d[k * 128:(k + 1) * 128, :]), "sW3", 16)
        for k in range(2):
            ctx.inc(eng.dma_start(out=g3[k], in_=g3_d[k * 128:(k + 1) * 128, :]), "sW3", 16)
        ctx.mark(("wffn3",), "sW3")

        ctx.wait(eng, ("ta", NT - 1, KD - 1))
        ctx.epoch(eng, "sW1")
        for k in range(KD):
            ctx.inc(eng.dma_start(out=w1[k], in_=w1_d[k * 128:(k + 1) * 128, :]), "sW1", 16)
        ctx.mark(("wffn1",), "sW1")
        ctx.wait(eng, ("tyall", NT - 1))
        ctx.epoch(eng, "sW2")
        ctx.epoch(eng, "sW3")
        for k in range(KD):
            ctx.inc(eng.dma_start(out=w2[k], in_=w2_d[k * 128:(k + 1) * 128, :]), "sW2", 16)
        for k in range(4):
            ctx.inc(eng.dma_start(out=w3[k], in_=w3_d[k * 128:(k + 1) * 128, :]), "sW2", 16)
        for k in range(2):
            ctx.inc(eng.dma_start(out=w4[k], in_=w4_d[k * 128:(k + 1) * 128, :]), "sW2", 16)
        ctx.mark(("wffn2",), "sW2")

        for j in range(NT + 2):
            if j >= 2:
                i = j - 2
                ctx.wait(eng, ("vfin", i))
                ctx.epoch(eng, "sOUT")
                ctx.inc(eng.dma_start(
                    out=rearr_d(out[seq_of(i), t0_of(i):t0_of(i) + TT, :]),
                    in_=x1b[i % 2].rearrange("p (j d) -> p j d", d=Dm)), "sOUT", 16)
                ctx.mark(("outd", i), "sOUT")
            if j < NT:
                i = j
                sx = "sX%d" % (i % 2)
                if i < 2:
                    ctx.wait(eng, ("spall", NT - 1))
                    ctx.wait(eng, ("spall", NT - 2))
                    ctx.wait(eng, ("vz", NT - 2 + i))
                else:
                    ctx.wait(eng, ("outd", i - 2))
                ctx.epoch(eng, sx)
                ctx.inc(eng.dma_start(
                    out=x1b[i % 2].rearrange("p (j d) -> p j d", d=Dm),
                    in_=rearr_d(x1_scr[tok0(i):tok0(i) + TT, :])), sx, 16)
                ctx.mark(("b2x", i), sx)

    def emit_sync(eng):
        for j in range(NT + 2):
            if j < NT:
                i = j
                st_ = "sYt%d" % (i % 2)
                ctx.wait(eng, ("lnwh", i))
                ctx.wait(eng, ("lnw", i))
                if i >= 2:
                    ctx.wait(eng, ("tyall", i - 2))
                ctx.epoch(eng, st_)
                for k in range(KD):
                    ctx.inc(eng.dma_start_transpose(
                        out=xfm[i % 2][:, k * TT:(k + 1) * TT],
                        in_=xln_scr[tok0(i):tok0(i) + TT, k * 128:(k + 1) * 128]),
                        st_, 16)
                ctx.mark(("tr", i), st_)
            if j >= 2:
                i = j - 2
                ss_ = "sYs%d" % (i % 2)
                ctx.epoch(eng, ss_)
                for s in range(NSUB):
                    ctx.wait(eng, ("vx1s", i, s))
                    ctx.inc(eng.dma_start(
                        out=x1_scr[tok0(i) + s * 128:tok0(i) + (s + 1) * 128, :],
                        in_=xb[i % 2][:, s * Dm:(s + 1) * Dm]), ss_, 16)
                ctx.mark(("spall", i), ss_)
            if j < NT:
                i = j
                sx = "sXA%d" % (i % 2)
                if i >= 2:
                    ctx.wait(eng, ("spall", i - 2))
                    ctx.wait(eng, ("vz", i - 2))
                    ctx.wait(eng, ("lx", i - 2))
                ctx.inc(eng.dma_start(
                    out=xb[i % 2].rearrange("p (j d) -> p j d", d=Dm),
                    in_=rearr_d(u[seq_of(i), t0_of(i):t0_of(i) + TT, :])), sx, 16)
                ctx.mark(("lx", i), sx)

        for j in range(NT + 2):
            if j < NT:
                i = j
                st_ = "sYt%d" % (i % 2)
                ctx.wait(eng, ("zw", i))
                if i >= 2:
                    ctx.wait(eng, ("tg0h0", i - 2))
                else:
                    ctx.wait(eng, ("tyall", NT - 2 + i))
                ctx.epoch(eng, st_)
                for k in range(KD):
                    ctx.inc(eng.dma_start_transpose(
                        out=zfm[i % 2][:, k * TT:(k + 1) * TT],
                        in_=z_scr[tok0(i):tok0(i) + TT, k * 128:(k + 1) * 128]),
                        st_, 16)
                ctx.mark(("ztr", i), st_)
        if ctx.emit:
            eng.wait_ge(ctx.handles["sOUT"], ctx.cnt["sOUT"])
            for s_ in ("sLN0", "sLN1", "sYs0", "sYs1"):
                eng.wait_ge(ctx.handles[s_], ctx.cnt[s_])

    def ln_combine(eng, meanout, rout, tag, prior_key):
        ctx.wait(eng, prior_key)
        v3 = bnraw.rearrange("p (s v) -> p s v", v=12)
        mcols = v3[:, :, 1:12:3]
        vcols = v3[:, :, 2:12:3]
        sq3 = stsq.rearrange("p (s v) -> p s v", v=4)
        ctx.chain(eng, eng.reduce_sum(out=st1[:, :], in_=mcols, axis=AXX))
        ctx.chain(eng, eng.tensor_scalar_mul(out=meanout[:, :], in0=st1[:, :], scalar1=0.25))
        ctx.chain(eng, eng.reduce_sum(out=st2[:, :], in_=vcols, axis=AXX))
        ctx.chain(eng, eng.tensor_mul(out=sq3, in0=mcols, in1=mcols))
        ctx.chain(eng, eng.reduce_sum(out=st1[:, :], in_=sq3, axis=AXX))
        ctx.chain(eng, eng.tensor_scalar_mul(out=st2[:, :], in0=st2[:, :], scalar1=1.0 / Dm))
        ctx.chain(eng, eng.scalar_tensor_tensor(out=svar[:, :], in0=st1[:, :], scalar=0.25,
                                                in1=st2[:, :], op0=ALU.mult, op1=ALU.add))
        ctx.chain(eng, eng.tensor_mul(out=st1[:, :], in0=meanout[:, :], in1=meanout[:, :]))
        ctx.chain(eng, eng.tensor_sub(out=svar[:, :], in0=svar[:, :], in1=st1[:, :]))
        ctx.inc(eng.tensor_scalar_add(out=svar[:, :], in0=svar[:, :], scalar1=EPS), "sV", 1)
        ctx.mark((tag, "vready"), "sV")
        ctx.wait(eng, (tag, "sqrt"))
        ctx.chain(eng, eng.reciprocal(out=rout[:, :], in_=ssq[:, :]))
        ctx.chain(eng, eng.tensor_mul(out=st1[:, :], in0=rout[:, :], in1=rout[:, :]))
        ctx.chain(eng, eng.tensor_mul(out=st1[:, :], in0=st1[:, :], in1=svar[:, :]))
        ctx.chain(eng, eng.tensor_scalar(out=st1[:, :], in0=st1[:, :], scalar1=-0.5,
                                         scalar2=1.5, op0=ALU.mult, op1=ALU.add))
        ctx.inc(eng.tensor_mul(out=rout[:, :], in0=rout[:, :], in1=st1[:, :]), "sV", 1)

    def emit_vector(eng):
        eng.memset(ones_f[:, :], 1.0)
        eng.memset(ones_bf[:, :], 1.0)
        eng.memset(ones_b1[:, :], 1.0)
        eng.memset(onesc_b[:, :], 1.0)
        eng.memset(id1f[:, :], 1.0)
        ctx.inc(eng.memset(id1b[:, :], 1.0), "sV", 1)
        ctx.mark(("consts",), "sV")
        ctx.wait(eng, ("smalls",))
        ctx.chain(eng, eng.tensor_scalar_mul(out=ba2[:, :], in0=bac[:, :], scalar1=0.5))
        ctx.chain(eng, eng.tensor_scalar_mul(out=b4h[0:1, 1:2], in0=b4h[0:1, 0:1], scalar1=0.5))
        ctx.inc(eng.tensor_scalar_mul(out=gb3h[0:1, 1:2], in0=gb3h[0:1, 0:1], scalar1=0.5), "sV", 1)
        ctx.mark(("bias2",), "sV")

        def comb1(i):
            c4 = slice(4 * i, 4 * i + 4)
            ctx.wait(eng, ("vbn1", i))
            raw = bnraw[:, i * 48:(i + 1) * 48].rearrange("p (s v) -> p s v", v=12)
            mcols = raw[:, :, 1:12:3]
            vcols = raw[:, :, 2:12:3]
            sq3 = stsq[:, 16 * i:16 * (i + 1)].rearrange("p (s v) -> p s v", v=4)
            ctx.chain(eng, eng.reduce_sum(out=st1[:, c4], in_=mcols, axis=AXX))
            ctx.chain(eng, eng.tensor_scalar_mul(out=mean1[:, c4], in0=st1[:, c4],
                                                 scalar1=0.25))
            ctx.chain(eng, eng.reduce_sum(out=st2[:, c4], in_=vcols, axis=AXX))
            ctx.chain(eng, eng.tensor_mul(out=sq3, in0=mcols, in1=mcols))
            ctx.chain(eng, eng.reduce_sum(out=st1[:, c4], in_=sq3, axis=AXX))
            ctx.chain(eng, eng.tensor_scalar_mul(out=st2[:, c4], in0=st2[:, c4],
                                                 scalar1=1.0 / Dm))
            ctx.chain(eng, eng.scalar_tensor_tensor(out=svar[:, c4], in0=st1[:, c4],
                                                    scalar=0.25, in1=st2[:, c4],
                                                    op0=ALU.mult, op1=ALU.add))
            ctx.chain(eng, eng.tensor_mul(out=st1[:, c4], in0=mean1[:, c4],
                                          in1=mean1[:, c4]))
            ctx.chain(eng, eng.tensor_sub(out=svar[:, c4], in0=svar[:, c4],
                                          in1=st1[:, c4]))
            ctx.inc(eng.tensor_scalar_add(out=svar[:, c4], in0=svar[:, c4],
                                          scalar1=EPS), "sV", 1)
            ctx.mark(("s1v", i), "sV")

        def nwtln1(i):
            c4 = slice(4 * i, 4 * i + 4)
            ctx.wait(eng, ("sqp", i))
            ctx.chain(eng, eng.reciprocal(out=r1[:, c4], in_=ssq[:, c4]))
            ctx.chain(eng, eng.tensor_mul(out=st1[:, c4], in0=r1[:, c4], in1=r1[:, c4]))
            ctx.chain(eng, eng.tensor_mul(out=st1[:, c4], in0=st1[:, c4],
                                          in1=svar[:, c4]))
            ctx.chain(eng, eng.tensor_scalar(out=st1[:, c4], in0=st1[:, c4],
                                             scalar1=-0.5, scalar2=1.5,
                                             op0=ALU.mult, op1=ALU.add))
            ctx.chain(eng, eng.tensor_mul(out=r1[:, c4], in0=r1[:, c4], in1=st1[:, c4]))
            if i >= 2:
                ctx.wait(eng, ("lnwh", i - 2))
                ctx.wait(eng, ("lnw", i - 2))
            for s in range(NSUB):
                gs = i * NSUB + s
                sub = bex[:, s * Dm:(s + 1) * Dm]
                last = eng.tensor_scalar(
                    out=xlnbf[i % 2][:, s * Dm:(s + 1) * Dm], in0=sub,
                    scalar1=mean1[:, gs:gs + 1], scalar2=r1[:, gs:gs + 1],
                    op0=ALU.subtract, op1=ALU.mult)
            ctx.inc(last, "sV", 1)
            ctx.mark(("vln", i), "sV")

        def bn1s(i):
            ctx.wait(eng, ("plx", i))
            for s in range(NSUB):
                gs = i * NSUB + s
                sub = bex[:, s * Dm:(s + 1) * Dm]
                eng.bn_stats(out=bnraw[:, gs * 12:gs * 12 + 6], in_=sub[:, 0:512])
                last = eng.bn_stats(out=bnraw[:, gs * 12 + 6:gs * 12 + 12],
                                    in_=sub[:, 512:1024])
            ctx.inc(last, "sV", 1)
            ctx.mark(("vbn1", i), "sV")

        bn1s(0)
        comb1(0)
        nwtln1(0)
        bn1s(1)
        comb1(1)
        nwtln1(1)

        for j in range(NT + 2):
            if 2 <= j:
                i = j - 2
                ctx.wait(eng, ("lx", i))
                for s in range(NSUB):
                    for h in range(2):
                        g = 2 * s + h
                        ctx.wait(eng, ("ty", i, s, h))
                        blk = xb[i % 2][:, s * Dm + h * 512:s * Dm + (h + 1) * 512]
                        ctx.inc(eng.tensor_add(out=blk, in0=blk, in1=pb[4 + g % 2][:, :]),
                                "sV", 1)
                        ctx.mark(("vx1g", i, g), "sV")
                        if h == 1:
                            ctx.mark(("vx1s", i, s), "sV")
                ctx.wait(eng, ("vx1g", i, 7))
                for s in range(NSUB):
                    gs = i * NSUB + s
                    sub = xb[i % 2][:, s * Dm:(s + 1) * Dm]
                    eng.bn_stats(out=bnraw[:, gs * 12:gs * 12 + 6], in_=sub[:, 0:512])
                    last = eng.bn_stats(out=bnraw[:, gs * 12 + 6:gs * 12 + 12],
                                        in_=sub[:, 512:1024])
                ctx.inc(last, "sV", 1)
                ctx.mark(("vbn2", i), "sV")
            if 1 <= j <= NT:
                i = j - 1

                def affine(k):
                    ctx.wait(eng, ("sa", i, k))
                    ctx.chain(eng, eng.tensor_scalar(
                        out=alf[:, k * TT:(k + 1) * TT],
                        in0=alf[:, k * TT:(k + 1) * TT],
                        scalar1=0.5, scalar2=0.5, op0=ALU.mult, op1=ALU.add))

                def scan(k):
                    ctx.wait(eng, ("tb", i, k))
                    ini = 0.0 if i % TPS == 0 else carry[:, k:k + 1]
                    ctx.inc(eng.tensor_tensor_scan(
                        out=hsbf[i % 2][:, k * TT:(k + 1) * TT],
                        data0=alf[:, k * TT:(k + 1) * TT],
                        data1=pb[2 + (k % 2)][:, :],
                        initial=ini, op0=ALU.mult, op1=ALU.add), "sV", 1)
                    ctx.mark(("vsc", i, k), "sV")
                if i >= 2:
                    ctx.wait(eng, ("tyall", i - 2))
                affine(0)
                for k in range(KD):
                    if k + 1 < KD:
                        affine(k + 1)
                    scan(k)
            if 1 <= j <= NT:
                i = j - 1
                ctx.wait(eng, ("vsc", i, KD - 1))
                ctx.inc(eng.engine_nop(), "sV", 1)
                ctx.mark(("vhs", i), "sV")
                if (i + 1) % TPS != 0:
                    ctx.inc(eng.tensor_copy(out=carry[:, :],
                                            in_=hsbf[i % 2][:, TT - 1::TT]), "sV", 1)
            if j + 2 < NT:
                bn1s(j + 2)
                comb1(j + 2)
            if 2 <= j:
                i = j - 2
                ctx.wait(eng, ("vbn2", i))
                c4 = slice(4 * i, 4 * i + 4)
                raw = bnraw[:, i * 48:(i + 1) * 48].rearrange("p (s v) -> p s v", v=12)
                mcols = raw[:, :, 1:12:3]
                vcols = raw[:, :, 2:12:3]
                sq3 = stsq[:, 16 * i:16 * (i + 1)].rearrange("p (s v) -> p s v", v=4)
                ctx.chain(eng, eng.reduce_sum(out=st1[:, c4], in_=mcols, axis=AXX))
                ctx.chain(eng, eng.tensor_scalar_mul(out=mean2[:, c4], in0=st1[:, c4],
                                                     scalar1=0.25))
                ctx.chain(eng, eng.reduce_sum(out=st2[:, c4], in_=vcols, axis=AXX))
                ctx.chain(eng, eng.tensor_mul(out=sq3, in0=mcols, in1=mcols))
                ctx.chain(eng, eng.reduce_sum(out=st1[:, c4], in_=sq3, axis=AXX))
                ctx.chain(eng, eng.tensor_scalar_mul(out=st2[:, c4], in0=st2[:, c4],
                                                     scalar1=1.0 / Dm))
                ctx.chain(eng, eng.scalar_tensor_tensor(out=svar[:, c4], in0=st1[:, c4],
                                                        scalar=0.25, in1=st2[:, c4],
                                                        op0=ALU.mult, op1=ALU.add))
                ctx.chain(eng, eng.tensor_mul(out=st1[:, c4], in0=mean2[:, c4],
                                              in1=mean2[:, c4]))
                ctx.chain(eng, eng.tensor_sub(out=svar[:, c4], in0=svar[:, c4],
                                              in1=st1[:, c4]))
                ctx.inc(eng.tensor_scalar_add(out=svar[:, c4], in0=svar[:, c4],
                                              scalar1=EPS), "sV", 1)
                ctx.mark(("s2v", i), "sV")
                ctx.wait(eng, ("sqz", i))
                ctx.chain(eng, eng.reciprocal(out=r2t[:, c4], in_=ssq[:, c4]))
                ctx.chain(eng, eng.tensor_mul(out=st1[:, c4], in0=r2t[:, c4],
                                              in1=r2t[:, c4]))
                ctx.chain(eng, eng.tensor_mul(out=st1[:, c4], in0=st1[:, c4],
                                              in1=svar[:, c4]))
                ctx.chain(eng, eng.tensor_scalar(out=st1[:, c4], in0=st1[:, c4],
                                                 scalar1=-0.5, scalar2=1.5,
                                                 op0=ALU.mult, op1=ALU.add))
                ctx.chain(eng, eng.tensor_mul(out=r2t[:, c4], in0=r2t[:, c4],
                                              in1=st1[:, c4]))
                if i >= 2:
                    ctx.wait(eng, ("zw", i - 2))
                    ctx.wait(eng, ("gz", i - 2))
                for s in range(NSUB):
                    gs = i * NSUB + s
                    sub = xb[i % 2][:, s * Dm:(s + 1) * Dm]
                    last = eng.tensor_scalar(
                        out=zsl[i % 2][:, s * Dm:(s + 1) * Dm], in0=sub,
                        scalar1=mean2[:, gs:gs + 1], scalar2=r2t[:, gs:gs + 1],
                        op0=ALU.subtract, op1=ALU.mult)
                ctx.inc(last, "sV", 1)
                ctx.mark(("vz", i), "sV")
            if j + 2 < NT:
                nwtln1(j + 2)

            if 2 <= j and (j - 2) % TPS == TPS - 1:
                i = j - 2
                b = seq_of(i)
                ctx.wait(eng, ("gacc", b))
                ctx.chain(eng, eng.tensor_scalar_mul(
                    out=gsb[0:1, b * Dm:b * Dm + 512], in0=pb[6][0:1, :],
                    scalar1=1.0 / T))
                ctx.inc(eng.tensor_scalar_mul(
                    out=gsb[0:1, b * Dm + 512:(b + 1) * Dm], in0=pb[7][0:1, :],
                    scalar1=1.0 / T), "sV", 1)
                ctx.mark(("gsb", b), "sV")
                if b == 1:
                    ctx.wait(eng, ("gfmT", 1))
                    ctx.inc(eng.tensor_copy(out=gfmb[:, 8:16],
                                            in_=pbb6[:, 16:32:2]), "sV", 1)
                    ctx.mark(("gfmb", 1), "sV")
                    ctx.wait(eng, ("tg", 0))
                    ctx.wait(eng, ("tg", 1))
                    ctx.chain(eng, eng.tensor_scalar(out=A2[:, :], in0=tg[:, :],
                                                     scalar1=arv[0:1, 0:1], scalar2=0.7,
                                                     op0=ALU.mult, op1=ALU.mult))
                    ctx.chain(eng, eng.tensor_scalar_add(out=A2[:, :], in0=A2[:, :],
                                                         scalar1=0.1))
                    ctx.inc(eng.tensor_scalar_mul(out=Bc[:, :], in0=arv[:, :],
                                                  scalar1=0.3), "sV", 1)
                    ctx.mark(("a2",), "sV")
            if j == 6:
                ctx.wait(eng, ("gfmT", 0))
                ctx.inc(eng.tensor_copy(out=gfmb[:, 0:8],
                                        in_=pbb6[:, 0:16:2]), "sV", 1)
                ctx.mark(("gfmb", 0), "sV")
        ctx.wait(eng, ("wffn1",))
        ctx.wait(eng, ("wffn2",))
        ctx.wait(eng, ("wffn3",))
        for k in range(KD):
            sgn = -1.0 if (k < 2 or k >= 6) else 1.0
            last = eng.tensor_scalar_mul(out=w1p[k], in0=w1[(k + 4) % KD], scalar1=sgn)
        ctx.inc(last, "sV", 1)
        ctx.mark(("w1p",), "sV")

        def vH0e(i):
            rn = i % 2
            for g in range(KD):
                ctx.wait(eng, ("tH0", i, g))
                ctx.inc(eng.tensor_copy(out=H0[rn][:, g * TT:(g + 1) * TT],
                                        in_=pb[g % 2][:, :]), "sV", 1)
                ctx.mark(("eH0", i, g), "sV")

        def vFinal(i, subs):
            o = cco[i % 2]
            a3, b3v = o[:, 12:16], o[:, 16:20]
            tmp = o[:, 20:24]
            if subs[0] == 0:
                ctx.wait(eng, ("b2x", i))
            for s in subs:
                gs = i * NSUB + s
                a3c, b3c_ = a3[:, s:s + 1], b3v[:, s:s + 1]
                r2c = r2t[:, gs:gs + 1]
                m2c = mean2[:, gs:gs + 1]
                P1, PB, PBn, CP, CMn = (cvf[:, 0:1], cvf[:, 1:2], cvf[:, 2:3],
                                        cvf[:, 3:4], cvf[:, 4:5])
                ctx.chain(eng, eng.tensor_mul(out=P1, in0=a3c, in1=r2c))
                ctx.chain(eng, eng.tensor_scalar_add(out=P1, in0=P1, scalar1=1.0))
                ctx.chain(eng, eng.tensor_mul(out=PB, in0=b3c_, in1=r2c))
                ctx.chain(eng, eng.tensor_scalar_mul(out=PBn, in0=PB, scalar1=-1.0))
                ctx.chain(eng, eng.tensor_mul(out=tmp[:, s:s + 1], in0=m2c, in1=r2c))
                ctx.chain(eng, eng.tensor_add(out=CP, in0=a3c, in1=b3c_))
                ctx.chain(eng, eng.tensor_mul(out=CP, in0=CP, in1=tmp[:, s:s + 1]))
                ctx.chain(eng, eng.tensor_scalar_mul(out=CP, in0=CP, scalar1=-1.0))
                ctx.chain(eng, eng.tensor_sub(out=CMn, in0=b3c_, in1=a3c))
                ctx.chain(eng, eng.tensor_mul(out=CMn, in0=CMn, in1=tmp[:, s:s + 1]))
                x1s = x1b[i % 2][:, s * Dm:(s + 1) * Dm]
                for (qa, qb) in ((0, 2), (1, 3)):
                    blka = x1s[:, qa * 256:(qa + 1) * 256]
                    blkb = x1s[:, qb * 256:(qb + 1) * 256]
                    ta = fsc[:, 0:256]
                    tb_ = fsc[:, 256:512]
                    if qa == 0:
                        ctx.chain(eng, eng.tensor_scalar(out=ta, in0=blkb, scalar1=PB,
                                                         scalar2=CP, op0=ALU.mult, op1=ALU.add))
                        ctx.chain(eng, eng.tensor_scalar(out=tb_, in0=blka, scalar1=PBn,
                                                         scalar2=CMn, op0=ALU.mult, op1=ALU.add))
                    else:
                        ctx.chain(eng, eng.tensor_scalar(out=ta, in0=blkb, scalar1=PBn,
                                                         scalar2=CMn, op0=ALU.mult, op1=ALU.add))
                        ctx.chain(eng, eng.tensor_scalar(out=tb_, in0=blka, scalar1=PB,
                                                         scalar2=CP, op0=ALU.mult, op1=ALU.add))
                    ctx.chain(eng, eng.scalar_tensor_tensor(out=blka, in0=blka, scalar=P1,
                                                            in1=ta, op0=ALU.mult, op1=ALU.add))
                    ctx.chain(eng, eng.scalar_tensor_tensor(out=blkb, in0=blkb, scalar=P1,
                                                            in1=tb_, op0=ALU.mult, op1=ALU.add))
            if subs[-1] == NSUB - 1:
                ctx.inc(eng.engine_nop(), "sV", 1)
                ctx.mark(("vfin", i), "sV")

        vH0e(0)
        for i in range(NT):
            r = i % 2
            b = seq_of(i)
            if i <= 1:
                ctx.wait(eng, ("zw", NT - 1))
                ctx.wait(eng, ("zw", NT - 2))
                ctx.wait(eng, ("gz", NT - 1))
            for cyc in (1, 2, 3):
                if cyc == 2:
                    ctx.wait(eng, ("cbe", i, 1))
                    ctx.wait(eng, ("eH0", i, KD - 1))
                    if i >= 2:
                        ctx.wait(eng, ("gel1", i - 2, 3, KD - 1))
                    for k in range(KD):
                        last = eng.tensor_mul(out=Gcur[r][:, k * TT:(k + 1) * TT],
                                              in0=C1b[r][:, :],
                                              in1=H0[r][:, k * TT:(k + 1) * TT])
                    ctx.chain(eng, last)
                    for k in range(KD):
                        ctx.inc(eng.tensor_add(out=Gcur[r][:, k * TT:(k + 1) * TT],
                                               in0=Gcur[r][:, k * TT:(k + 1) * TT],
                                               in1=G0[r][:, k * TT:(k + 1) * TT]),
                                "sV", 1)
                        ctx.mark(("gck", i, 2, k), "sV")
                    ctx.mark(("gc", i, 2), "sV")
                    if i >= 1:
                        vFinal(i - 1, (2, 3))
                if cyc == 3:
                    c1r = cf[r][0:1, 0:512]
                    c2r = cf[r][0:1, 512:1024]
                    p2r = cf[r][0:1, 1536:2048]
                    q2r = cf[r][0:1, 2048:2560]
                    ctx.wait(eng, ("cfm", i, 2))
                    ctx.chain(eng, eng.tensor_mul(out=p2r, in0=c1r, in1=c2r))
                    ctx.chain(eng, eng.tensor_scalar(out=p2r, in0=p2r, scalar1=-1.0,
                                                     scalar2=1.0, op0=ALU.mult, op1=ALU.add))
                    ctx.inc(eng.tensor_add(out=q2r, in0=c1r, in1=c2r), "sV", 1)
                    ctx.mark(("p2q2", i), "sV")
                    ctx.wait(eng, ("cbe", i, 3))
                    ctx.wait(eng, ("th2", i, 2, 3))
                    for k in range(KD):
                        eng.tensor_mul(out=Gcur[r][:, k * TT:(k + 1) * TT],
                                       in0=P2b[r][:, :], in1=G0[r][:, k * TT:(k + 1) * TT])
                    for k in range(KD):
                        last = eng.tensor_mul(out=h1t[r][:, k * TT:(k + 1) * TT],
                                              in0=Q2b[r][:, :],
                                              in1=H0[r][:, k * TT:(k + 1) * TT])
                    ctx.chain(eng, last)
                    for k in range(KD):
                        ctx.inc(eng.tensor_add(out=Gcur[r][:, k * TT:(k + 1) * TT],
                                               in0=Gcur[r][:, k * TT:(k + 1) * TT],
                                               in1=h1t[r][:, k * TT:(k + 1) * TT]),
                                "sV", 1)
                        ctx.mark(("gck", i, 3, k), "sV")
                    ctx.mark(("gc", i, 3), "sV")
                    if i + 1 < NT:
                        vH0e(i + 1)
                ctx.wait(eng, ("t4", i, cyc))
                ctx.wait(eng, ("a2",))
                if cyc == 1 and i >= 2:
                    ctx.wait(eng, ("ctT", i - 2))
                    ctx.wait(eng, ("tcb", i - 2, 3))
                ccol = cf[r][0:1, (cyc - 1) * 512:cyc * 512]
                ctx.inc(eng.tensor_scalar(
                    out=ccol, in0=t4b[r][0:1, :], scalar1=Bc[0:1, 0:1],
                    scalar2=A2[0:1, b:b + 1], op0=ALU.mult, op1=ALU.add), "sV", 1)
                ctx.mark(("cfm", i, cyc), "sV")

            ctx.wait(eng, ("ctT", i))
            ctx.inc(eng.tensor_copy(out=ctv[r][:, :], in_=pbb6[:, 0:24:2]), "sV", 1)
            ctx.mark(("ctv", i), "sV")
            ctx.wait(eng, ("ctv", i))
            cc1, cc2, cc3 = ctv[r][:, 0:4], ctv[r][:, 4:8], ctv[r][:, 8:12]
            o = cco[r]
            q12, q13, q23 = o[:, 0:4], o[:, 4:8], o[:, 8:12]
            a3, b3v = o[:, 12:16], o[:, 16:20]
            tmp = o[:, 20:24]
            ctx.chain(eng, eng.tensor_mul(out=q12, in0=cc1, in1=cc2))
            eng.tensor_mul(out=q13, in0=cc1, in1=cc3)
            ctx.chain(eng, eng.tensor_mul(out=q23, in0=cc2, in1=cc3))
            ctx.chain(eng, eng.tensor_add(out=a3, in0=q12, in1=q13))
            ctx.chain(eng, eng.tensor_add(out=a3, in0=a3, in1=q23))
            ctx.chain(eng, eng.tensor_scalar(out=a3, in0=a3, scalar1=-1.0, scalar2=1.0,
                                             op0=ALU.mult, op1=ALU.add))
            ctx.chain(eng, eng.tensor_mul(out=tmp, in0=q12, in1=cc3))
            ctx.chain(eng, eng.tensor_add(out=b3v, in0=cc1, in1=cc2))
            ctx.chain(eng, eng.tensor_add(out=b3v, in0=b3v, in1=cc3))
            ctx.chain(eng, eng.tensor_sub(out=b3v, in0=b3v, in1=tmp))
            vFinal(i, (0, 1))
        vFinal(NT - 1, (2, 3))

    def emit_scalar(eng):
        def sqp(i):
            ctx.wait(eng, ("s1v", i))
            ctx.inc(eng.activation(out=ssq[:, 4 * i:4 * i + 4],
                                   in_=svar[:, 4 * i:4 * i + 4], func=AF.Sqrt,
                                   bias=0.0, scale=1.0), "sS", 1)
            ctx.mark(("sqp", i), "sS")

        def sqz(i):
            ctx.wait(eng, ("s2v", i))
            ctx.inc(eng.activation(out=ssq[:, 4 * i:4 * i + 4],
                                   in_=svar[:, 4 * i:4 * i + 4], func=AF.Sqrt,
                                   bias=0.0, scale=1.0), "sS", 1)
            ctx.mark(("sqz", i), "sS")

        def gacts(b):
            for m in range(4):
                ctx.wait(eng, ("gc1", b, m))
                ctx.inc(eng.activation(out=hg1[:, b * 4 + m:b * 4 + m + 1],
                                       in_=pb[6][:, 64 + b * 4 + m:65 + b * 4 + m],
                                       func=AF.Gelu,
                                       bias=gb1c[:, m:m + 1], scale=1.0), "sS", 1)
            ctx.mark(("hg1", b), "sS")
            for m in range(2):
                ctx.wait(eng, ("gc2", b, m))
                ctx.inc(eng.activation(out=hg2[:, b * 2 + m:b * 2 + m + 1],
                                       in_=pb[6][:, 80 + b * 2 + m:81 + b * 2 + m],
                                       func=AF.Gelu, bias=gb2c[:, m:m + 1], scale=1.0),
                        "sS", 1)
            ctx.mark(("hg2", b), "sS")
            ctx.wait(eng, ("gc3", b))
            ctx.inc(eng.activation(out=tg[0:1, b:b + 1], in_=pb[6][0:1, 96 + b:97 + b],
                                   func=AF.Tanh, bias=gb3h[0:1, 1:2], scale=0.5), "sS", 1)
            ctx.mark(("tg", b), "sS")

        def spillx(i):
            sl = "sLN%d" % (i % 2)
            sb = "sLB%d" % (i % 2)
            src3 = xlnbf[i % 2].rearrange("p (j d) -> p j d", d=Dm)
            ctx.wait(eng, ("vln", i))
            ctx.epoch(eng, sl)
            ctx.inc(eng.dma_start(
                out=xln_scr[tok0(i):tok0(i) + TT, 0:512].rearrange(
                    "(j p) d -> p j d", p=128),
                in_=src3[:, :, 0:512]), sl, 16)
            ctx.mark(("lnwh", i), sl)
            ctx.epoch(eng, sb)
            ctx.inc(eng.dma_start(
                out=xln_scr[tok0(i):tok0(i) + TT, 512:1024].rearrange(
                    "(j p) d -> p j d", p=128),
                in_=src3[:, :, 512:1024]), sb, 16)
            ctx.mark(("lnw", i), sb)

        def spillz(i):
            sb = "sLB%d" % (i % 2)
            ctx.wait(eng, ("vz", i))
            ctx.epoch(eng, sb)
            ctx.inc(eng.dma_start(
                out=z_scr[tok0(i):tok0(i) + TT, :].rearrange(
                    "(j p) d -> p j d", p=128),
                in_=zsl[i % 2].rearrange("p (j d) -> p j d", d=Dm)), sb, 16)
            ctx.mark(("zw", i), sb)

        sqp(0)
        spillx(0)
        sqp(1)
        spillx(1)
        ctx.wait(eng, ("bias2",))

        for i in range(NT):
            if i + 2 < NT:
                sqp(i + 2)
            for m in range(KD):
                ctx.wait(eng, ("ta", i, m))
                if i >= 1:
                    ctx.wait(eng, ("vsc", i - 1, m))
                ctx.inc(eng.activation(out=alf[:, m * TT:(m + 1) * TT],
                                       in_=pb[m % 2][:, :], func=AF.Tanh,
                                       bias=ba2[:, m:m + 1], scale=0.5), "sS", 1)
                ctx.mark(("sa", i, m), "sS")
            if i >= 1:
                sqz(i - 1)
                spillz(i - 1)
            if i + 2 < NT:
                spillx(i + 2)
            if i == 6:
                gacts(0)
        sqz(NT - 1)
        spillz(NT - 1)
        gacts(1)
        def sG0e(i):
            r = i % 2
            if i == 1:
                ctx.wait(eng, ("zw", NT - 1))
                ctx.wait(eng, ("zw", NT - 2))
                ctx.wait(eng, ("gz", NT - 1))
            if i >= 2:
                ctx.wait(eng, ("gc", i - 2, 3))
                ctx.wait(eng, ("gel1", i - 2, 3, KD - 1))
            for g in range(KD):
                ctx.wait(eng, ("tG0", i, g))
                ctx.inc(eng.copy(out=G0[r][:, g * TT:(g + 1) * TT], in_=pb[g % 2][:, :]),
                        "sS", 1)
                ctx.mark(("eG0", i, g), "sS")

        def sCyc(i, cyc):
            r = i % 2
            if cyc == 1 and i >= 2:
                ctx.wait(eng, ("th2", i - 2, 3, 3))
                ctx.wait(eng, ("th3", i - 2, 3, 1))
                ctx.wait(eng, ("ttf", i - 2, 3))
                ctx.wait(eng, ("cfm", i - 2, 3))
            if cyc == 3:
                ctx.wait(eng, ("tcb", i, 3))
                eng.copy(out=P2b[r][:, :], in_=pb[3][:, :])
                ctx.inc(eng.copy(out=Q2b[r][:, :], in_=pb[5][:, :]), "sS", 1)
                ctx.mark(("cbe", i, 3), "sS")
            src_ = G0[r] if cyc == 1 else Gcur[r]
            if i <= 1 and cyc == 1:
                ctx.wait(eng, ("zw", NT - 1))
                ctx.wait(eng, ("zw", NT - 2))
                ctx.wait(eng, ("gz", NT - 1))
            if cyc == 1:
                ctx.wait(eng, ("eG0", i, KD - 1))
            for k in range(KD):
                if cyc > 1:
                    ctx.wait(eng, ("gck", i, cyc, k))
                ctx.inc(eng.activation(out=h1t[r][:, k * TT:(k + 1) * TT],
                                       in_=src_[:, k * TT:(k + 1) * TT], func=AF.Gelu,
                                       bias=b1c[:, k:k + 1], scale=1.0), "sS", 1)
                ctx.mark(("gel1", i, cyc, k), "sS")
            for m in range(4):
                ctx.wait(eng, ("th2", i, cyc, m))
                ctx.inc(eng.activation(out=h2t[r][:, m * TT:(m + 1) * TT],
                                       in_=pb[m % 2][:, :], func=AF.Gelu,
                                       bias=b2c[:, m:m + 1], scale=1.0), "sS", 1)
                ctx.mark(("gel2", i, cyc, m), "sS")
            for m in range(2):
                ctx.wait(eng, ("th3", i, cyc, m))
                ctx.inc(eng.activation(out=h3t[r][:, m * TT:(m + 1) * TT],
                                       in_=pb[m % 2][:, :], func=AF.Gelu,
                                       bias=b3c[:, m:m + 1], scale=1.0), "sS", 1)
                ctx.mark(("gel3", i, cyc, m), "sS")
            ctx.wait(eng, ("ttf", i, cyc))
            if cyc > 1:
                ctx.wait(eng, ("cfm", i, cyc - 1))
            elif i > 0:
                ctx.wait(eng, ("cfm", i - 1, 3))
            ctx.inc(eng.activation(out=t4b[r][0:1, :], in_=pb[2][0:1, :],
                                   func=AF.Tanh, bias=b4h[0:1, 1:2], scale=0.5), "sS", 1)
            ctx.mark(("t4", i, cyc), "sS")
            if cyc == 1:
                ctx.wait(eng, ("tcb", i, 1))
                ctx.inc(eng.copy(out=C1b[r][:, :], in_=pb[3][:, :]), "sS", 1)
                ctx.mark(("cbe", i, 1), "sS")

        for i in range(NT):
            if i >= 1:
                sCyc(i - 1, 1)
            sG0e(i)
            if i >= 1:
                sCyc(i - 1, 2)
                sCyc(i - 1, 3)
        sCyc(NT - 1, 1)
        sCyc(NT - 1, 2)
        sCyc(NT - 1, 3)
    def emit_tensor(eng):
        ctx.wait(eng, ("wattn1",))

        def zmean(zi):
            b = seq_of(zi)
            first = (zi % TPS == 0)
            last_t = (zi % TPS == TPS - 1)
            if zi == TPS:
                ctx.wait(eng, ("tg", 0))
            ctx.wait(eng, ("vz", zi))
            for s in range(NSUB):
                for h in range(2):
                    mm = eng.matmul(pb[6 + h][0:1, :], onesc_b[:, :],
                                    zsl[zi % 2][:, s * Dm + h * 512:s * Dm + (h + 1) * 512],
                                    start=(first and s == 0),
                                    stop=(last_t and s == NSUB - 1),
                                    skip_group_check=True)
            ctx.inc(mm, "sT", 1)
            ctx.mark(("gz", zi), "sT")
            if last_t:
                ctx.mark(("gacc", b), "sT")

        def gfmT(b):
            ctx.wait(eng, ("gsb", b))
            for k in range(KD):
                c_ = 2 * (b * 8 + k)
                mm = eng.matmul(pbb6[:, c_:c_ + 1],
                                gsb[0:1, b * Dm + k * 128:b * Dm + (k + 1) * 128],
                                id1b[0:1, 0:1], start=(k == 0), stop=(k == KD - 1),
                                is_transpose=True, skip_group_check=True)
            ctx.inc(mm, "sT", 1)
            ctx.mark(("gfmT", b), "sT")

        def gcpart(b):
            ctx.wait(eng, ("wffn3",))
            ctx.wait(eng, ("gfmb", b))
            for m in range(4):
                for k in range(KD):
                    mm = eng.matmul(pb[6][:, 64 + b * 4 + m:65 + b * 4 + m],
                                    g1[k][:, m * 128:(m + 1) * 128],
                                    gfmb[:, b * 8 + k:b * 8 + k + 1],
                                    start=False, stop=(k == KD - 1),
                                    skip_group_check=True)
                ctx.inc(mm, "sT", 1)
                ctx.mark(("gc1", b, m), "sT")
            ctx.wait(eng, ("hg1", b))
            for m in range(2):
                for k in range(4):
                    mm = eng.matmul(pb[6][:, 80 + b * 2 + m:81 + b * 2 + m],
                                    g2[k][:, m * 128:(m + 1) * 128],
                                    hg1[:, b * 4 + k:b * 4 + k + 1],
                                    start=False, stop=(k == 3),
                                    skip_group_check=True)
                ctx.inc(mm, "sT", 1)
                ctx.mark(("gc2", b, m), "sT")
            ctx.wait(eng, ("hg2", b))
            for k in range(2):
                mm = eng.matmul(pb[6][0:1, 96 + b:97 + b], g3[k][0:128, 0:1],
                                hg2[:, b * 2 + k:b * 2 + k + 1],
                                start=False, stop=(k == 1), skip_group_check=True)
            ctx.inc(mm, "sT", 1)
            ctx.mark(("gc3", b), "sT")

        def yblk(yi):
            ctx.wait(eng, ("vhs", yi))
            if yi == 0:
                ctx.wait(eng, ("dstrl",))
                ctx.wait(eng, ("wattn3",))
            for s in range(NSUB):
                for h in range(2):
                    g = 2 * s + h
                    if yi > 0 or g >= 2:
                        pi, pg = (yi, g - 2) if g >= 2 else (yi - 1, g + 6)
                        ctx.wait(eng, ("vx1g", pi, pg))
                    for k in range(KD):
                        eng.matmul(
                            pb[4 + g % 2][:, :],
                            hsbf[yi % 2][:, k * TT + s * 128:k * TT + (s + 1) * 128],
                            cT[k][:, h * 512:(h + 1) * 512],
                            start=(k == 0), stop=False)
                    for kr in range(4):
                        kdm = 4 * h + kr
                        mm = eng.matmul(
                            pb[4 + g % 2][:, :],
                            xfm[yi % 2][:, kdm * TT + s * 128:kdm * TT + (s + 1) * 128],
                            dstrb[:, (h * 4 + kr) * 512:(h * 4 + kr + 1) * 512],
                            start=False, stop=(kr == 3))
                    ctx.inc(mm, "sT", 1)
                    ctx.mark(("ty", yi, s, h), "sT")
            ctx.mark(("tyall", yi), "sT")

        for i in range(NT):
            ctx.wait(eng, ("tr", i))
            for m in range(KD):
                if i > 0 or m >= 2:
                    pi, pm = (i, m - 2) if m >= 2 else (i - 1, m + 6)
                    ctx.wait(eng, ("sa", pi, pm))
                for k in range(KD):
                    mm = eng.matmul(pb[m % 2][:, :], waT[k][:, m * 128:(m + 1) * 128],
                                    xfm[i % 2][:, k * TT:(k + 1) * TT],
                                    start=(k == 0), stop=(k == KD - 1))
                ctx.inc(mm, "sT", 1)
                ctx.mark(("ta", i, m), "sT")
            if 2 <= i <= 5:
                zmean(i - 2)
            if i == 7:
                zmean(4)
                zmean(5)
            if i == 6:
                gfmT(0)
            if i > 0:
                yblk(i - 1)
            if i == 0:
                ctx.wait(eng, ("wattn2",))
                ctx.wait(eng, ("smalls",))
                ctx.wait(eng, ("consts",))
            for m in range(KD):
                if i > 0 or m >= 2:
                    pi, pm = (i, m - 2) if m >= 2 else (i - 1, m + 6)
                    ctx.wait(eng, ("vsc", pi, pm))
                for k in range(KD):
                    eng.matmul(pb[2 + m % 2][:, :], wbT[k][:, m * 128:(m + 1) * 128],
                               xfm[i % 2][:, k * TT:(k + 1) * TT],
                               start=(k == 0), stop=False)
                mm = eng.matmul(pb[2 + m % 2][:, :], bbrow[0:1, m * 128:(m + 1) * 128],
                                ones_b1[0:1, :], start=False, stop=True)
                ctx.inc(mm, "sT", 1)
                ctx.mark(("tb", i, m), "sT")
            ctx.mark(("tmma", i), "sT")
            if i == 6:
                gcpart(0)
            if i == NT - 1:
                yblk(i)
                zmean(6)
                zmean(7)
                gfmT(1)
                gcpart(1)

        bankq = [None, None]

        def bigmm(consumer_mark, emitfn):
            j = bigmm.idx
            bigmm.idx += 1
            bk = j % 2
            if bankq[bk] is not None:
                ctx.wait(eng, bankq[bk])
            emitfn(pb[bk])
            bankq[bk] = consumer_mark
        bigmm.idx = 0

        ctx.wait(eng, ("w1p",))
        ctx.wait(eng, ("wffn1",))
        ctx.wait(eng, ("wffn2",))
        def tG0(i):
            r = i % 2
            ctx.wait(eng, ("ztr", i))
            for g in range(KD):
                def f(bank, g=g, i=i, r=r):
                    for k in range(KD):
                        mm = eng.matmul(bank[:, :], w1[k][:, g * 128:(g + 1) * 128],
                                        zfm[r][:, k * TT:(k + 1) * TT],
                                        start=(k == 0), stop=(k == KD - 1))
                    ctx.inc(mm, "sT", 1)
                    ctx.mark(("tG0", i, g), "sT")
                bigmm(("eG0", i, g), f)

        def tH0(i):
            r = i % 2
            for g in range(KD):
                def f(bank, g=g, i=i, r=r):
                    for k in range(KD):
                        mm = eng.matmul(bank[:, :], w1p[k][:, g * 128:(g + 1) * 128],
                                        zfm[r][:, k * TT:(k + 1) * TT],
                                        start=(k == 0), stop=(k == KD - 1))
                    ctx.inc(mm, "sT", 1)
                    ctx.mark(("tH0", i, g), "sT")
                bigmm(("eH0", i, g), f)
            ctx.mark(("tg0h0", i), "sT")

        def tCyc(i, cyc):
            r = i % 2
            if True:
                for m in range(4):
                    def f(bank, m=m, i=i, r=r, cyc=cyc):
                        for k in range(KD):
                            if m == 0:
                                ctx.wait(eng, ("gel1", i, cyc, k))
                            mm = eng.matmul(bank[:, :], w2[k][:, m * 128:(m + 1) * 128],
                                            h1t[r][:, k * TT:(k + 1) * TT],
                                            start=(k == 0), stop=(k == KD - 1))
                        ctx.inc(mm, "sT", 1)
                        ctx.mark(("th2", i, cyc, m), "sT")
                    bigmm(("gel2", i, cyc, m), f)
                for m in range(2):
                    def f(bank, m=m, i=i, r=r, cyc=cyc):
                        for k in range(4):
                            if m == 0:
                                ctx.wait(eng, ("gel2", i, cyc, k))
                            mm = eng.matmul(bank[:, :], w3[k][:, m * 128:(m + 1) * 128],
                                            h2t[r][:, k * TT:(k + 1) * TT],
                                            start=(k == 0), stop=(k == 3))
                        ctx.inc(mm, "sT", 1)
                        ctx.mark(("th3", i, cyc, m), "sT")
                    bigmm(("gel3", i, cyc, m), f)
                if i == 0 and cyc == 1:
                    ctx.wait(eng, ("tg", 0))
                    ctx.wait(eng, ("tg", 1))
                else:
                    ctx.wait(eng, ("t4", i, cyc - 1) if cyc > 1 else ("t4", i - 1, 3))
                for k in range(2):
                    ctx.wait(eng, ("gel3", i, cyc, k))
                    mm = eng.matmul(pb[2][0:1, :], w4[k][0:128, 0:1],
                                    h3t[r][:, k * TT:(k + 1) * TT],
                                    start=(k == 0), stop=(k == 1))
                ctx.inc(mm, "sT", 1)
                ctx.mark(("ttf", i, cyc), "sT")
                if cyc == 1:
                    ctx.wait(eng, ("cfm", i, 1))
                    if i > 0:
                        ctx.wait(eng, ("cbe", i - 1, 3))
                    ctx.inc(eng.matmul(pb[3][:, :], ones_bf[:, :], cf[r][0:1, 0:512],
                                       start=True, stop=True), "sT", 1)
                    ctx.mark(("tcb", i, 1), "sT")
                ctx.wait(eng, ("cfm", i, cyc))
                if cyc == 1 and i >= 1:
                    ctx.wait(eng, ("ctv", i - 1))
                for s in range(NSUB):
                    c_ = 2 * (4 * (cyc - 1) + s)
                    mm = eng.matmul(
                        pbb6[:, c_:c_ + 1],
                        cf[r][0:1, (cyc - 1) * 512 + s * 128:(cyc - 1) * 512 + (s + 1) * 128],
                        id1b[0:1, 0:1],
                        start=(cyc == 1 and s == 0), stop=(cyc == 3 and s == 3),
                        is_transpose=True, skip_group_check=True)
                if cyc == 3:
                    ctx.inc(mm, "sT", 1)
                    ctx.mark(("ctT", i), "sT")
                if cyc == 2:
                    ctx.wait(eng, ("p2q2", i))
                    ctx.wait(eng, ("cbe", i, 1))
                    if i > 0:
                        ctx.wait(eng, ("cbe", i - 1, 3))
                    eng.matmul(pb[3][:, :], ones_bf[:, :], cf[r][0:1, 1536:2048],
                               start=True, stop=True)
                    ctx.inc(eng.matmul(pb[5][:, :], ones_bf[:, :], cf[r][0:1, 2048:2560],
                                       start=True, stop=True), "sT", 1)
                    ctx.mark(("tcb", i, 3), "sT")


        for i in range(NT):
            if i >= 1:
                tCyc(i - 1, 1)
            tG0(i)
            if i >= 1:
                tCyc(i - 1, 2)
            tH0(i)
            if i >= 1:
                tCyc(i - 1, 3)
        tCyc(NT - 1, 1)
        tCyc(NT - 1, 2)
        tCyc(NT - 1, 3)
    ctx.reset(False)
    dummy = _DummyEng()
    emit_gpsimd(dummy)
    emit_sync(dummy)
    emit_vector(dummy)
    emit_scalar(dummy)
    emit_tensor(dummy)

    import contextlib
    with contextlib.ExitStack() as stack:
        for s in Ctx.SEMS:
            ctx.handles[s] = stack.enter_context(nc.semaphore(s))
        block = stack.enter_context(nc.Block())
        ctx.reset(True)
        block.gpsimd(lambda e: emit_gpsimd(e))
        block.sync(lambda e: emit_sync(e))
        block.vector(lambda e: emit_vector(e))
        block.scalar(lambda e: emit_scalar(e))
        block.tensor(lambda e: emit_tensor(e))
    return nc


def kernel(**inputs):
    u = np.ascontiguousarray(inputs["L0_attn_u"])

    def chunked(v, nk):
        return np.ascontiguousarray(np.asarray(v).reshape(nk, 128).T)

    def dstrips(D_):
        ds = np.zeros((8, 128, 512), np.float32)
        idx = np.arange(128)
        for h in range(2):
            for kr in range(4):
                ds[h * 4 + kr, idx, kr * 128 + idx] = D_[(4 * h + kr) * 128:
                                                         (4 * h + kr + 1) * 128]
        return ds

    shared = {
        "waT": np.ascontiguousarray(np.asarray(inputs["L0_attn_W_alpha_w"]).T),
        "wbT": np.ascontiguousarray(np.asarray(inputs["L0_attn_W_B_w"]).T),
        "cT": np.ascontiguousarray(np.asarray(inputs["L0_attn_C_w"]).T),
        "w1": np.ascontiguousarray(inputs["L0_ffn_cn_w1"]),
        "w2": np.ascontiguousarray(inputs["L0_ffn_cn_w2"]),
        "w3": np.ascontiguousarray(inputs["L0_ffn_cn_w3"]),
        "w4": np.ascontiguousarray(inputs["L0_ffn_cn_w4"]),
        "g1": np.ascontiguousarray(inputs["L0_ffn_gc_w1"]),
        "g2": np.ascontiguousarray(inputs["L0_ffn_gc_w2"]),
        "g3": np.ascontiguousarray(inputs["L0_ffn_gc_w3"]),
        "ba": chunked(inputs["L0_attn_W_alpha_b"], 8),
        "bbrow_d": np.ascontiguousarray(np.asarray(inputs["L0_attn_W_B_b"]).reshape(1, Dm)),
        "b1": chunked(inputs["L0_ffn_cn_b1"], 8),
        "b2": chunked(inputs["L0_ffn_cn_b2"], 4),
        "b3": chunked(inputs["L0_ffn_cn_b3"], 2),
        "b4": np.ascontiguousarray(np.asarray(inputs["L0_ffn_cn_b4"]).reshape(1, 1)),
        "gb1": chunked(inputs["L0_ffn_gc_b1"], 4),
        "gb2": chunked(inputs["L0_ffn_gc_b2"], 2),
        "gb3": np.ascontiguousarray(np.asarray(inputs["L0_ffn_gc_b3"]).reshape(1, 1)),
        "ar": np.ascontiguousarray(np.asarray(inputs["L0_ffn_adaptive_range"]).reshape(1, 1)),
        "dstr": dstrips(np.asarray(inputs["L0_attn_D"]).reshape(Dm)),
    }
    nc = build_program()
    in_maps = []
    for c in range(NCORES):
        m = dict(shared)
        m["u"] = np.ascontiguousarray(u[SEQ * c:SEQ * (c + 1)])
        in_maps.append(m)
    res = run_bass_kernel_spmd(nc, in_maps, core_ids=list(range(NCORES)))
    return np.concatenate([np.asarray(r["out"]) for r in res.results], axis=0)


if __name__ == "__main__":
    build_program()
    print("built OK")


# version 20 (final)
# speedup vs baseline: 1.0065x; 1.0065x over previous
import numpy as np
import concourse.bass as bass
import concourse.mybir as mybir
from concourse.bass_utils import run_bass_kernel_spmd

F32 = mybir.dt.float32
BF16 = mybir.dt.bfloat16
AF = mybir.ActivationFunctionType
ALU = mybir.AluOpType
AXX = mybir.AxisListType.X

B, T, Dm = 16, 2048, 1024
NCORES = 8
SEQ = B // NCORES
TPC = SEQ * T
TT = 512
NT = TPC // TT
TPS = NT // SEQ
NSUB = TT // 128
KD = Dm // 128
NSUBS = NT * NSUB
EPS = 1e-5


class _DummyInstr:
    def then_inc(self, *a, **k):
        return self


class _DummyEng:
    def __getattr__(self, name):
        def f(*a, **kw):
            return _DummyInstr()
        return f


class Ctx:
    SEMS = ["sW1", "sW2", "sW3", "sGs", "sX0", "sX1", "sLN0", "sLN1",
            "sLB0", "sLB1", "sYt0", "sYt1", "sYs0", "sYs1", "sOUT",
            "sT", "sV", "sS", "sVt", "sTS0", "sTS1", "sXA0", "sXA1"]

    def __init__(self):
        self.emit = False
        self.handles = {}
        self.cnt = {s: 0 for s in self.SEMS}
        self.marks = {}

    def reset(self, emit):
        self.emit = emit
        self.cnt = {s: 0 for s in self.SEMS}

    def inc(self, instr, sem, n=1):
        self.cnt[sem] += n
        if self.emit:
            instr.then_inc(self.handles[sem], n)
        return instr

    def mark(self, key, sem):
        self.marks[key] = (sem, self.cnt[sem])

    def wait(self, eng, key):
        sem, val = self.marks.get(key, (None, 0))
        if self.emit and val > 0:
            eng.wait_ge(self.handles[sem], val)

    def chain(self, eng, instr):
        self.inc(instr, "sVt", 1)
        if self.emit:
            eng.wait_ge(self.handles["sVt"], self.cnt["sVt"])

    def epoch(self, eng, sem):
        if self.emit and self.cnt[sem] > 0:
            eng.wait_ge(self.handles[sem], self.cnt[sem])


def build_program():
    nc = bass.Bass("TRN2", target_bir_lowering=False)

    u = nc.dram_tensor("u", [SEQ, T, Dm], F32, kind="ExternalInput").ap()
    DI = lambda n, s: nc.dram_tensor(n, list(s), F32, kind="ExternalInput").ap()
    waT_d = DI("waT", (Dm, Dm))
    wbT_d = DI("wbT", (Dm, Dm))
    cT_d = DI("cT", (Dm, Dm))
    w1_d = DI("w1", (Dm, Dm))
    w2_d = DI("w2", (Dm, 512))
    w3_d = DI("w3", (512, 256))
    w4_d = DI("w4", (256, 1))
    g1_d = DI("g1", (Dm, 512))
    g2_d = DI("g2", (512, 256))
    g3_d = DI("g3", (256, 1))
    ba_d = DI("ba", (128, 8))
    bbrow_d = DI("bbrow_d", (1, Dm))
    b1_d = DI("b1", (128, 8))
    b2_d = DI("b2", (128, 4))
    b3_d = DI("b3", (128, 2))
    b4_d = DI("b4", (1, 1))
    gb1_d = DI("gb1", (128, 4))
    gb2_d = DI("gb2", (128, 2))
    gb3_d = DI("gb3", (1, 1))
    ar_d = DI("ar", (1, 1))
    dstr_d = DI("dstr", (8, 128, 512))
    out = nc.dram_tensor("out", [SEQ, T, Dm], F32, kind="ExternalOutput").ap()

    xln_scr = nc.dram_tensor("xln_scr", [TPC, Dm], BF16, kind="Internal").ap()
    x1_scr = nc.dram_tensor("x1_scr", [TPC, Dm], F32, kind="Internal").ap()
    z_scr = nc.dram_tensor("z_scr", [TPC, Dm], BF16, kind="Internal").ap()

    A = lambda name, shape, dt, **kw: nc.alloc_sbuf_tensor(name, shape, dt, **kw).ap()

    WB = A("WB", [128, 27008], BF16)
    waT = [WB[:, k * 1024:(k + 1) * 1024] for k in range(KD)]
    wbT = [WB[:, 8192 + k * 1024:8192 + (k + 1) * 1024] for k in range(KD)]
    cT = [WB[:, 16384 + k * 1024:16384 + (k + 1) * 1024] for k in range(KD)]
    w1 = [WB[:, k * 1024:(k + 1) * 1024] for k in range(KD)]
    w1p = [WB[:, 8192 + k * 1024:8192 + (k + 1) * 1024] for k in range(KD)]
    w2 = [WB[:, 16384 + k * 512:16384 + (k + 1) * 512] for k in range(KD)]
    w3 = [WB[:, 20480 + k * 256:20480 + (k + 1) * 256] for k in range(4)]
    w4 = [WB[:, 21504 + k:21504 + k + 1] for k in range(2)]


    dstrb = A("dstrb", [128, KD * 512], BF16)
    ba2 = A("ba2", [128, 8], F32)
    bac = A("bac", [128, 8], F32)
    b1c = A("b1c", [128, 8], F32)
    b2c = A("b2c", [128, 4], F32)
    b3c = A("b3c", [128, 2], F32)
    b4h = A("b4h", [1, 2], F32)
    gb1c = A("gb1c", [128, 4], F32)
    gb2c = A("gb2c", [128, 2], F32)
    gb3h = A("gb3h", [1, 2], F32)
    arv = A("arv", [1, 1], F32)
    bbrow = A("bbrow", [1, Dm], BF16)
    ones_f = A("ones_f", [1, 128], F32)
    ones_bf = A("ones_bf", [1, 128], BF16)
    ones_b1 = A("ones_b1", [1, TT], BF16)
    onesc_b = A("onesc_b", [128, 1], BF16)
    id1f = A("id1f", [1, 1], F32)
    id1b = A("id1b", [1, 1], BF16)

    bnraw = A("bnraw", [128, NSUBS * 12], F32)
    mean1 = A("mean1", [128, NSUBS], F32)
    r1 = A("r1", [128, NSUBS], F32)
    mean2 = A("mean2", [128, NSUBS], F32)
    r2t = A("r2t", [128, NSUBS], F32)
    svar = A("svar", [128, NSUBS], F32)
    ssq = A("ssq", [128, NSUBS], F32)
    st1 = A("st1", [128, NSUBS], F32)
    st2 = A("st2", [128, NSUBS], F32)
    stsq = A("stsq", [128, NSUBS * 4], F32)
    carry = A("carry", [128, 8], F32)

    xb = [A("xb0", [128, NSUB * Dm], F32), A("xb1", [128, NSUB * Dm], F32)]
    xlnbf = [A("xlnbf0", [128, NSUB * Dm], BF16), A("xlnbf1", [128, NSUB * Dm], BF16)]
    xfm = [A("xfm0", [128, KD * TT], BF16, is_transpose_dest=True),
           A("xfm1", [128, KD * TT], BF16, is_transpose_dest=True)]
    alf = A("alf", [128, KD * TT], F32)
    hsf = A("hsf", [128, KD * TT], F32)
    hsbf = [A("hsbf0", [128, KD * TT], BF16), A("hsbf1", [128, KD * TT], BF16)]
    bex = A("bex", [128, 4096], F32)
    bex2 = A("bex2", [128, 640], F32)
    fsc = A("fsc", [128, 512], F32)
    cvf = A("cvf", [128, 5], F32)

    x1b = xb
    zbf = xlnbf
    zfm = xfm
    alf_b = alf.bitcast(BF16)
    hsf_b = hsf.bitcast(BF16)
    zsl = [hsf_b[:, 0:4096], hsf_b[:, 4096:8192]]
    G0 = [alf_b[:, 0:4096], hsf_b[:, 0:4096]]
    H0 = [alf_b[:, 4096:8192], hsf_b[:, 4096:8192]]
    Gcur = [xlnbf[0][:, 0:4096], hsbf[0][:, 0:4096]]
    h1t = [xlnbf[1][:, 0:4096], hsbf[1][:, 0:4096]]
    bex_b = bex.bitcast(BF16)
    g1 = [bex_b[:, k * 512:(k + 1) * 512] for k in range(KD)]
    g2 = [bex_b[:, 4096 + k * 256:4096 + (k + 1) * 256] for k in range(4)]
    g3 = [bex_b[:, 5120 + k:5120 + k + 1] for k in range(2)]
    h2t = [bex_b[:, 0:2048], bex_b[:, 2048:4096]]
    h3t = [bex_b[:, 4096:5120], bex_b[:, 5120:6144]]
    C1b = [bex_b[:, 6144:6656], bex_b[:, 6656:7168]]
    P2b = [bex_b[:, 7168:7680], bex_b[:, 7680:8192]]
    bex2_b = bex2.bitcast(BF16)
    Q2b = [bex2_b[:, 0:512], bex2_b[:, 512:1024]]
    gfmb = bex2_b[:, 1024:1040]
    hg1 = bex2_b[:, 1040:1048]
    hg2 = bex2_b[:, 1048:1052]
    ctv = [A("ctv0", [128, 12], F32), A("ctv1", [128, 12], F32)]
    cco = [A("cco0", [128, 24], F32), A("cco1", [128, 24], F32)]
    cf_ = A("cf0", [1, 2560], BF16)
    cf = [cf_, cf_]
    t4b_ = A("t4b", [1, TT], F32)
    t4b = [t4b_, t4b_]
    gsb = cf_[0:1, 0:2 * Dm]
    tg = st2[0:1, 0:2]
    A2 = st2[0:1, 2:4]
    Bc = st2[0:1, 4:5]

    pb = [nc.alloc_psum_tensor("pb%d" % i, [128, 512], F32).ap() for i in range(8)]
    pbb2 = pb[2].bitcast(BF16)
    pbb6 = pb[6].bitcast(BF16)

    assert nc.sbuf_bytes_remaining >= 0

    ctx = Ctx()
    _li = [_DummyInstr()]

    def seq_of(i):
        return i // TPS

    def t0_of(i):
        return (i % TPS) * TT

    def tok0(i):
        return i * TT

    def rearr_d(d2):
        return d2.rearrange("(j p) d -> p j d", p=128)

    def emit_gpsimd(eng):
        for dst, src in [(bac, ba_d), (b1c, b1_d), (b2c, b2_d), (b3c, b3_d),
                         (gb1c, gb1_d), (gb2c, gb2_d)]:
            ctx.inc(eng.dma_start(out=dst[:, :], in_=src), "sGs", 16)
        ctx.inc(eng.dma_start(out=b4h[0:1, 0:1], in_=b4_d), "sGs", 16)
        ctx.inc(eng.dma_start(out=gb3h[0:1, 0:1], in_=gb3_d), "sGs", 16)
        ctx.inc(eng.dma_start(out=arv[:, :], in_=ar_d), "sGs", 16)
        ctx.inc(eng.dma_start(out=bbrow[:, :], in_=bbrow_d), "sGs", 16)
        ctx.mark(("smalls",), "sGs")

        def plxload(i):
            sx = "sX%d" % (i % 2)
            if i >= 1:
                ctx.wait(eng, ("vln", i - 1))
            ctx.epoch(eng, sx)
            ctx.inc(eng.dma_start(
                out=bex.rearrange("p (j d) -> p j d", d=Dm),
                in_=rearr_d(u[seq_of(i), t0_of(i):t0_of(i) + TT, :])), sx, 16)
            ctx.mark(("plx", i), sx)

        plxload(0)
        ctx.inc(eng.dma_start(
            out=WB[:, 0:8192].rearrange("p (k c) -> p k c", c=1024),
            in_=waT_d.rearrange("(k p) c -> p k c", p=128)), "sW1", 16)
        ctx.mark(("wattn1",), "sW1")
        ctx.epoch(eng, "sGs")
        ctx.inc(eng.dma_start(out=dstrb.rearrange("p (q c) -> p q c", c=512),
                              in_=dstr_d.rearrange("q p c -> p q c")), "sGs", 16)
        ctx.mark(("dstrl",), "sGs")
        plxload(1)
        ctx.inc(eng.dma_start(
            out=WB[:, 8192:16384].rearrange("p (k c) -> p k c", c=1024),
            in_=wbT_d.rearrange("(k p) c -> p k c", p=128)), "sW2", 16)
        ctx.mark(("wattn2",), "sW2")
        plxload(2)
        ctx.inc(eng.dma_start(
            out=WB[:, 16384:24576].rearrange("p (k c) -> p k c", c=1024),
            in_=cT_d.rearrange("(k p) c -> p k c", p=128)), "sW3", 16)
        ctx.mark(("wattn3",), "sW3")
        for i in range(3, NT):
            plxload(i)
        ctx.wait(eng, ("vln", NT - 1))
        ctx.epoch(eng, "sW3")
        for k in range(KD):
            ctx.inc(eng.dma_start(out=g1[k], in_=g1_d[k * 128:(k + 1) * 128, :]), "sW3", 16)
        for k in range(4):
            ctx.inc(eng.dma_start(out=g2[k], in_=g2_d[k * 128:(k + 1) * 128, :]), "sW3", 16)
        for k in range(2):
            ctx.inc(eng.dma_start(out=g3[k], in_=g3_d[k * 128:(k + 1) * 128, :]), "sW3", 16)
        ctx.mark(("wffn3",), "sW3")

        ctx.wait(eng, ("ta", NT - 1, KD - 1))
        ctx.epoch(eng, "sW1")
        ctx.inc(eng.dma_start(
            out=WB[:, 0:8192].rearrange("p (k c) -> p k c", c=1024),
            in_=w1_d.rearrange("(k p) c -> p k c", p=128)), "sW1", 16)
        ctx.mark(("wffn1",), "sW1")
        ctx.wait(eng, ("tyall", NT - 1))
        ctx.epoch(eng, "sW2")
        ctx.epoch(eng, "sW3")
        ctx.inc(eng.dma_start(
            out=WB[:, 16384:20480].rearrange("p (k c) -> p k c", c=512),
            in_=w2_d.rearrange("(k p) c -> p k c", p=128)), "sW2", 16)
        ctx.inc(eng.dma_start(
            out=WB[:, 20480:21504].rearrange("p (k c) -> p k c", c=256),
            in_=w3_d.rearrange("(k p) c -> p k c", p=128)), "sW2", 16)
        for k in range(2):
            ctx.inc(eng.dma_start(out=w4[k], in_=w4_d[k * 128:(k + 1) * 128, :]), "sW2", 16)
        ctx.mark(("wffn2",), "sW2")

        for j in range(NT + 2):
            if j >= 2:
                i = j - 2
                ctx.wait(eng, ("vfin", i))
                ctx.epoch(eng, "sOUT")
                ctx.inc(eng.dma_start(
                    out=rearr_d(out[seq_of(i), t0_of(i):t0_of(i) + TT, :]),
                    in_=x1b[i % 2].rearrange("p (j d) -> p j d", d=Dm)), "sOUT", 16)
                ctx.mark(("outd", i), "sOUT")
            if j < NT:
                i = j
                sx = "sX%d" % (i % 2)
                if i < 2:
                    ctx.wait(eng, ("spall", NT - 1))
                    ctx.wait(eng, ("spall", NT - 2))
                    ctx.wait(eng, ("vz", NT - 2 + i))
                else:
                    ctx.wait(eng, ("outd", i - 2))
                ctx.epoch(eng, sx)
                ctx.inc(eng.dma_start(
                    out=x1b[i % 2].rearrange("p (j d) -> p j d", d=Dm),
                    in_=rearr_d(x1_scr[tok0(i):tok0(i) + TT, :])), sx, 16)
                ctx.mark(("b2x", i), sx)

    def emit_sync(eng):
        for j in range(NT + 2):
            if j < NT:
                i = j
                st_ = "sYt%d" % (i % 2)
                ctx.wait(eng, ("lnwh", i))
                ctx.wait(eng, ("lnw", i))
                if i >= 2:
                    ctx.wait(eng, ("tyall", i - 2))
                ctx.epoch(eng, st_)
                for k in range(KD):
                    ctx.inc(eng.dma_start_transpose(
                        out=xfm[i % 2][:, k * TT:(k + 1) * TT],
                        in_=xln_scr[tok0(i):tok0(i) + TT, k * 128:(k + 1) * 128]),
                        st_, 16)
                ctx.mark(("tr", i), st_)
            if j >= 2:
                i = j - 2
                ss_ = "sYs%d" % (i % 2)
                ctx.epoch(eng, ss_)
                for s in range(NSUB):
                    ctx.wait(eng, ("vx1s", i, s))
                    ctx.inc(eng.dma_start(
                        out=x1_scr[tok0(i) + s * 128:tok0(i) + (s + 1) * 128, :],
                        in_=xb[i % 2][:, s * Dm:(s + 1) * Dm]), ss_, 16)
                ctx.mark(("spall", i), ss_)
            if j < NT:
                i = j
                sx = "sXA%d" % (i % 2)
                if i >= 2:
                    ctx.wait(eng, ("spall", i - 2))
                    ctx.wait(eng, ("vz", i - 2))
                    ctx.wait(eng, ("lx", i - 2))
                ctx.inc(eng.dma_start(
                    out=xb[i % 2].rearrange("p (j d) -> p j d", d=Dm),
                    in_=rearr_d(u[seq_of(i), t0_of(i):t0_of(i) + TT, :])), sx, 16)
                ctx.mark(("lx", i), sx)

        for j in range(NT + 2):
            if j < NT:
                i = j
                st_ = "sYt%d" % (i % 2)
                ctx.wait(eng, ("zw", i))
                if i >= 2:
                    ctx.wait(eng, ("tg0h0", i - 2))
                else:
                    ctx.wait(eng, ("tyall", NT - 2 + i))
                ctx.epoch(eng, st_)
                for k in range(KD):
                    ctx.inc(eng.dma_start_transpose(
                        out=zfm[i % 2][:, k * TT:(k + 1) * TT],
                        in_=z_scr[tok0(i):tok0(i) + TT, k * 128:(k + 1) * 128]),
                        st_, 16)
                ctx.mark(("ztr", i), st_)
        if ctx.emit:
            eng.wait_ge(ctx.handles["sOUT"], ctx.cnt["sOUT"])
            for s_ in ("sLN0", "sLN1", "sYs0", "sYs1"):
                eng.wait_ge(ctx.handles[s_], ctx.cnt[s_])

    def ln_combine(eng, meanout, rout, tag, prior_key):
        ctx.wait(eng, prior_key)
        v3 = bnraw.rearrange("p (s v) -> p s v", v=12)
        mcols = v3[:, :, 1:12:3]
        vcols = v3[:, :, 2:12:3]
        sq3 = stsq.rearrange("p (s v) -> p s v", v=4)
        ctx.chain(eng, eng.reduce_sum(out=st1[:, :], in_=mcols, axis=AXX))
        ctx.chain(eng, eng.tensor_scalar_mul(out=meanout[:, :], in0=st1[:, :], scalar1=0.25))
        ctx.chain(eng, eng.reduce_sum(out=st2[:, :], in_=vcols, axis=AXX))
        ctx.chain(eng, eng.tensor_mul(out=sq3, in0=mcols, in1=mcols))
        ctx.chain(eng, eng.reduce_sum(out=st1[:, :], in_=sq3, axis=AXX))
        ctx.chain(eng, eng.tensor_scalar_mul(out=st2[:, :], in0=st2[:, :], scalar1=1.0 / Dm))
        ctx.chain(eng, eng.scalar_tensor_tensor(out=svar[:, :], in0=st1[:, :], scalar=0.25,
                                                in1=st2[:, :], op0=ALU.mult, op1=ALU.add))
        ctx.chain(eng, eng.tensor_mul(out=st1[:, :], in0=meanout[:, :], in1=meanout[:, :]))
        ctx.chain(eng, eng.tensor_sub(out=svar[:, :], in0=svar[:, :], in1=st1[:, :]))
        ctx.inc(eng.tensor_scalar_add(out=svar[:, :], in0=svar[:, :], scalar1=EPS), "sV", 1)
        ctx.mark((tag, "vready"), "sV")
        ctx.wait(eng, (tag, "sqrt"))
        ctx.chain(eng, eng.reciprocal(out=rout[:, :], in_=ssq[:, :]))
        ctx.chain(eng, eng.tensor_mul(out=st1[:, :], in0=rout[:, :], in1=rout[:, :]))
        ctx.chain(eng, eng.tensor_mul(out=st1[:, :], in0=st1[:, :], in1=svar[:, :]))
        ctx.chain(eng, eng.tensor_scalar(out=st1[:, :], in0=st1[:, :], scalar1=-0.5,
                                         scalar2=1.5, op0=ALU.mult, op1=ALU.add))
        ctx.inc(eng.tensor_mul(out=rout[:, :], in0=rout[:, :], in1=st1[:, :]), "sV", 1)

    def emit_vector(eng):
        eng.memset(ones_f[:, :], 1.0)
        eng.memset(ones_bf[:, :], 1.0)
        eng.memset(ones_b1[:, :], 1.0)
        eng.memset(onesc_b[:, :], 1.0)
        eng.memset(id1f[:, :], 1.0)
        ctx.inc(eng.memset(id1b[:, :], 1.0), "sV", 1)
        ctx.mark(("consts",), "sV")
        ctx.wait(eng, ("smalls",))
        ctx.chain(eng, eng.tensor_scalar_mul(out=ba2[:, :], in0=bac[:, :], scalar1=0.5))
        ctx.chain(eng, eng.tensor_scalar_mul(out=b4h[0:1, 1:2], in0=b4h[0:1, 0:1], scalar1=0.5))
        ctx.inc(eng.tensor_scalar_mul(out=gb3h[0:1, 1:2], in0=gb3h[0:1, 0:1], scalar1=0.5), "sV", 1)
        ctx.mark(("bias2",), "sV")

        def comb1(i):
            c4 = slice(4 * i, 4 * i + 4)
            ctx.wait(eng, ("vbn1", i))
            raw = bnraw[:, i * 48:(i + 1) * 48].rearrange("p (s v) -> p s v", v=12)
            mcols = raw[:, :, 1:12:3]
            vcols = raw[:, :, 2:12:3]
            sq3 = stsq[:, 16 * i:16 * (i + 1)].rearrange("p (s v) -> p s v", v=4)
            ctx.chain(eng, eng.reduce_sum(out=st1[:, c4], in_=mcols, axis=AXX))
            ctx.chain(eng, eng.tensor_scalar_mul(out=mean1[:, c4], in0=st1[:, c4],
                                                 scalar1=0.25))
            ctx.chain(eng, eng.reduce_sum(out=st2[:, c4], in_=vcols, axis=AXX))
            ctx.chain(eng, eng.tensor_mul(out=sq3, in0=mcols, in1=mcols))
            ctx.chain(eng, eng.reduce_sum(out=st1[:, c4], in_=sq3, axis=AXX))
            ctx.chain(eng, eng.tensor_scalar_mul(out=st2[:, c4], in0=st2[:, c4],
                                                 scalar1=1.0 / Dm))
            ctx.chain(eng, eng.scalar_tensor_tensor(out=svar[:, c4], in0=st1[:, c4],
                                                    scalar=0.25, in1=st2[:, c4],
                                                    op0=ALU.mult, op1=ALU.add))
            ctx.chain(eng, eng.tensor_mul(out=st1[:, c4], in0=mean1[:, c4],
                                          in1=mean1[:, c4]))
            ctx.chain(eng, eng.tensor_sub(out=svar[:, c4], in0=svar[:, c4],
                                          in1=st1[:, c4]))
            ctx.inc(eng.tensor_scalar_add(out=svar[:, c4], in0=svar[:, c4],
                                          scalar1=EPS), "sV", 1)
            ctx.mark(("s1v", i), "sV")

        def nwtln1(i):
            c4 = slice(4 * i, 4 * i + 4)
            ctx.wait(eng, ("sqp", i))
            ctx.chain(eng, eng.reciprocal(out=r1[:, c4], in_=ssq[:, c4]))
            ctx.chain(eng, eng.tensor_mul(out=st1[:, c4], in0=r1[:, c4], in1=r1[:, c4]))
            ctx.chain(eng, eng.tensor_mul(out=st1[:, c4], in0=st1[:, c4],
                                          in1=svar[:, c4]))
            ctx.chain(eng, eng.tensor_scalar(out=st1[:, c4], in0=st1[:, c4],
                                             scalar1=-0.5, scalar2=1.5,
                                             op0=ALU.mult, op1=ALU.add))
            ctx.chain(eng, eng.tensor_mul(out=r1[:, c4], in0=r1[:, c4], in1=st1[:, c4]))
            if i >= 2:
                ctx.wait(eng, ("lnwh", i - 2))
                ctx.wait(eng, ("lnw", i - 2))
            for s in range(NSUB):
                gs = i * NSUB + s
                sub = bex[:, s * Dm:(s + 1) * Dm]
                last = eng.tensor_scalar(
                    out=xlnbf[i % 2][:, s * Dm:(s + 1) * Dm], in0=sub,
                    scalar1=mean1[:, gs:gs + 1], scalar2=r1[:, gs:gs + 1],
                    op0=ALU.subtract, op1=ALU.mult)
            ctx.inc(last, "sV", 1)
            ctx.mark(("vln", i), "sV")

        def bn1s(i):
            ctx.wait(eng, ("plx", i))
            for s in range(NSUB):
                gs = i * NSUB + s
                sub = bex[:, s * Dm:(s + 1) * Dm]
                eng.bn_stats(out=bnraw[:, gs * 12:gs * 12 + 6], in_=sub[:, 0:512])
                last = eng.bn_stats(out=bnraw[:, gs * 12 + 6:gs * 12 + 12],
                                    in_=sub[:, 512:1024])
            ctx.inc(last, "sV", 1)
            ctx.mark(("vbn1", i), "sV")

        bn1s(0)
        comb1(0)
        nwtln1(0)
        bn1s(1)
        comb1(1)
        nwtln1(1)

        for j in range(NT + 2):
            if 2 <= j:
                i = j - 2
                ctx.wait(eng, ("lx", i))
                for s in range(NSUB):
                    for h in range(2):
                        g = 2 * s + h
                        ctx.wait(eng, ("ty", i, s, h))
                        blk = xb[i % 2][:, s * Dm + h * 512:s * Dm + (h + 1) * 512]
                        ctx.inc(eng.tensor_add(out=blk, in0=blk, in1=pb[4 + g % 2][:, :]),
                                "sV", 1)
                        ctx.mark(("vx1g", i, g), "sV")
                        if h == 1:
                            ctx.mark(("vx1s", i, s), "sV")
                ctx.wait(eng, ("vx1g", i, 7))
                for s in range(NSUB):
                    gs = i * NSUB + s
                    sub = xb[i % 2][:, s * Dm:(s + 1) * Dm]
                    eng.bn_stats(out=bnraw[:, gs * 12:gs * 12 + 6], in_=sub[:, 0:512])
                    last = eng.bn_stats(out=bnraw[:, gs * 12 + 6:gs * 12 + 12],
                                        in_=sub[:, 512:1024])
                ctx.inc(last, "sV", 1)
                ctx.mark(("vbn2", i), "sV")
            if 1 <= j <= NT:
                i = j - 1

                def affine(k):
                    ctx.wait(eng, ("sa", i, k))
                    ctx.chain(eng, eng.tensor_scalar(
                        out=alf[:, k * TT:(k + 1) * TT],
                        in0=alf[:, k * TT:(k + 1) * TT],
                        scalar1=0.5, scalar2=0.5, op0=ALU.mult, op1=ALU.add))

                def scan(k):
                    ctx.wait(eng, ("tb", i, k))
                    ini = 0.0 if i % TPS == 0 else carry[:, k:k + 1]
                    ctx.inc(eng.tensor_tensor_scan(
                        out=hsbf[i % 2][:, k * TT:(k + 1) * TT],
                        data0=alf[:, k * TT:(k + 1) * TT],
                        data1=pb[2 + (k % 2)][:, :],
                        initial=ini, op0=ALU.mult, op1=ALU.add), "sV", 1)
                    ctx.mark(("vsc", i, k), "sV")
                if i >= 2:
                    ctx.wait(eng, ("tyall", i - 2))
                affine(0)
                for k in range(KD):
                    if k + 1 < KD:
                        affine(k + 1)
                    scan(k)
            if 1 <= j <= NT:
                i = j - 1
                ctx.wait(eng, ("vsc", i, KD - 1))
                ctx.inc(eng.engine_nop(), "sV", 1)
                ctx.mark(("vhs", i), "sV")
                if (i + 1) % TPS != 0:
                    ctx.inc(eng.tensor_copy(out=carry[:, :],
                                            in_=hsbf[i % 2][:, TT - 1::TT]), "sV", 1)
            if 2 <= j:
                i = j - 2
                ctx.wait(eng, ("vbn2", i))
                c4 = slice(4 * i, 4 * i + 4)
                raw = bnraw[:, i * 48:(i + 1) * 48].rearrange("p (s v) -> p s v", v=12)
                mcols = raw[:, :, 1:12:3]
                vcols = raw[:, :, 2:12:3]
                sq3 = stsq[:, 16 * i:16 * (i + 1)].rearrange("p (s v) -> p s v", v=4)
                ctx.chain(eng, eng.reduce_sum(out=st1[:, c4], in_=mcols, axis=AXX))
                ctx.chain(eng, eng.tensor_scalar_mul(out=mean2[:, c4], in0=st1[:, c4],
                                                     scalar1=0.25))
                ctx.chain(eng, eng.reduce_sum(out=st2[:, c4], in_=vcols, axis=AXX))
                ctx.chain(eng, eng.tensor_mul(out=sq3, in0=mcols, in1=mcols))
                ctx.chain(eng, eng.reduce_sum(out=st1[:, c4], in_=sq3, axis=AXX))
                ctx.chain(eng, eng.tensor_scalar_mul(out=st2[:, c4], in0=st2[:, c4],
                                                     scalar1=1.0 / Dm))
                ctx.chain(eng, eng.scalar_tensor_tensor(out=svar[:, c4], in0=st1[:, c4],
                                                        scalar=0.25, in1=st2[:, c4],
                                                        op0=ALU.mult, op1=ALU.add))
                ctx.chain(eng, eng.tensor_mul(out=st1[:, c4], in0=mean2[:, c4],
                                              in1=mean2[:, c4]))
                ctx.chain(eng, eng.tensor_sub(out=svar[:, c4], in0=svar[:, c4],
                                              in1=st1[:, c4]))
                ctx.inc(eng.tensor_scalar_add(out=svar[:, c4], in0=svar[:, c4],
                                              scalar1=EPS), "sV", 1)
                ctx.mark(("s2v", i), "sV")
                ctx.wait(eng, ("sqz", i))
                ctx.chain(eng, eng.reciprocal(out=r2t[:, c4], in_=ssq[:, c4]))
                ctx.chain(eng, eng.tensor_mul(out=st1[:, c4], in0=r2t[:, c4],
                                              in1=r2t[:, c4]))
                ctx.chain(eng, eng.tensor_mul(out=st1[:, c4], in0=st1[:, c4],
                                              in1=svar[:, c4]))
                ctx.chain(eng, eng.tensor_scalar(out=st1[:, c4], in0=st1[:, c4],
                                                 scalar1=-0.5, scalar2=1.5,
                                                 op0=ALU.mult, op1=ALU.add))
                ctx.chain(eng, eng.tensor_mul(out=r2t[:, c4], in0=r2t[:, c4],
                                              in1=st1[:, c4]))
                if i >= 2:
                    ctx.wait(eng, ("zw", i - 2))
                    ctx.wait(eng, ("gz", i - 2))
                for s in range(NSUB):
                    gs = i * NSUB + s
                    sub = xb[i % 2][:, s * Dm:(s + 1) * Dm]
                    last = eng.tensor_scalar(
                        out=zsl[i % 2][:, s * Dm:(s + 1) * Dm], in0=sub,
                        scalar1=mean2[:, gs:gs + 1], scalar2=r2t[:, gs:gs + 1],
                        op0=ALU.subtract, op1=ALU.mult)
                ctx.inc(last, "sV", 1)
                ctx.mark(("vz", i), "sV")
            if j + 2 < NT:
                bn1s(j + 2)
                comb1(j + 2)
                nwtln1(j + 2)

            if 2 <= j and (j - 2) % TPS == TPS - 1:
                i = j - 2
                b = seq_of(i)
                ctx.wait(eng, ("gacc", b))
                ctx.chain(eng, eng.tensor_scalar_mul(
                    out=gsb[0:1, b * Dm:b * Dm + 512], in0=pb[6][0:1, :],
                    scalar1=1.0 / T))
                ctx.inc(eng.tensor_scalar_mul(
                    out=gsb[0:1, b * Dm + 512:(b + 1) * Dm], in0=pb[7][0:1, :],
                    scalar1=1.0 / T), "sV", 1)
                ctx.mark(("gsb", b), "sV")
                if b == 1:
                    ctx.wait(eng, ("gfmT", 1))
                    ctx.inc(eng.tensor_copy(out=gfmb[:, 8:16],
                                            in_=pbb6[:, 16:32:2]), "sV", 1)
                    ctx.mark(("gfmb", 1), "sV")
                    ctx.wait(eng, ("tg", 0))
                    ctx.wait(eng, ("tg", 1))
                    ctx.chain(eng, eng.tensor_scalar(out=A2[:, :], in0=tg[:, :],
                                                     scalar1=arv[0:1, 0:1], scalar2=0.7,
                                                     op0=ALU.mult, op1=ALU.mult))
                    ctx.chain(eng, eng.tensor_scalar_add(out=A2[:, :], in0=A2[:, :],
                                                         scalar1=0.1))
                    ctx.inc(eng.tensor_scalar_mul(out=Bc[:, :], in0=arv[:, :],
                                                  scalar1=0.3), "sV", 1)
                    ctx.mark(("a2",), "sV")
            if j == 6:
                ctx.wait(eng, ("gfmT", 0))
                ctx.inc(eng.tensor_copy(out=gfmb[:, 0:8],
                                        in_=pbb6[:, 0:16:2]), "sV", 1)
                ctx.mark(("gfmb", 0), "sV")
        ctx.wait(eng, ("wffn1",))
        ctx.wait(eng, ("wffn2",))
        ctx.wait(eng, ("wffn3",))
        for k in range(KD):
            sgn = -1.0 if (k < 2 or k >= 6) else 1.0
            last = eng.tensor_scalar_mul(out=w1p[k], in0=w1[(k + 4) % KD], scalar1=sgn)
        ctx.inc(last, "sV", 1)
        ctx.mark(("w1p",), "sV")

        def vH0e(i):
            rn = i % 2
            for g in range(KD):
                ctx.wait(eng, ("tH0", i, g))
                ctx.inc(eng.tensor_copy(out=H0[rn][:, g * TT:(g + 1) * TT],
                                        in_=pb[g % 2][:, :]), "sV", 1)
                ctx.mark(("eH0", i, g), "sV")

        def vFinal(i, subs):
            o = cco[i % 2]
            a3, b3v = o[:, 12:16], o[:, 16:20]
            tmp = o[:, 20:24]
            if subs[0] == 0:
                ctx.wait(eng, ("b2x", i))
            for s in subs:
                gs = i * NSUB + s
                a3c, b3c_ = a3[:, s:s + 1], b3v[:, s:s + 1]
                r2c = r2t[:, gs:gs + 1]
                m2c = mean2[:, gs:gs + 1]
                P1, PB, PBn, CP, CMn = (cvf[:, 0:1], cvf[:, 1:2], cvf[:, 2:3],
                                        cvf[:, 3:4], cvf[:, 4:5])
                ctx.chain(eng, eng.tensor_mul(out=P1, in0=a3c, in1=r2c))
                ctx.chain(eng, eng.tensor_scalar_add(out=P1, in0=P1, scalar1=1.0))
                ctx.chain(eng, eng.tensor_mul(out=PB, in0=b3c_, in1=r2c))
                ctx.chain(eng, eng.tensor_scalar_mul(out=PBn, in0=PB, scalar1=-1.0))
                ctx.chain(eng, eng.tensor_mul(out=tmp[:, s:s + 1], in0=m2c, in1=r2c))
                ctx.chain(eng, eng.tensor_add(out=CP, in0=a3c, in1=b3c_))
                ctx.chain(eng, eng.tensor_mul(out=CP, in0=CP, in1=tmp[:, s:s + 1]))
                ctx.chain(eng, eng.tensor_scalar_mul(out=CP, in0=CP, scalar1=-1.0))
                ctx.chain(eng, eng.tensor_sub(out=CMn, in0=b3c_, in1=a3c))
                ctx.chain(eng, eng.tensor_mul(out=CMn, in0=CMn, in1=tmp[:, s:s + 1]))
                x1s = x1b[i % 2][:, s * Dm:(s + 1) * Dm]
                for (qa, qb) in ((0, 2), (1, 3)):
                    blka = x1s[:, qa * 256:(qa + 1) * 256]
                    blkb = x1s[:, qb * 256:(qb + 1) * 256]
                    ta = fsc[:, 0:256]
                    tb_ = fsc[:, 256:512]
                    if qa == 0:
                        ctx.chain(eng, eng.tensor_scalar(out=ta, in0=blkb, scalar1=PB,
                                                         scalar2=CP, op0=ALU.mult, op1=ALU.add))
                        ctx.chain(eng, eng.tensor_scalar(out=tb_, in0=blka, scalar1=PBn,
                                                         scalar2=CMn, op0=ALU.mult, op1=ALU.add))
                    else:
                        ctx.chain(eng, eng.tensor_scalar(out=ta, in0=blkb, scalar1=PBn,
                                                         scalar2=CMn, op0=ALU.mult, op1=ALU.add))
                        ctx.chain(eng, eng.tensor_scalar(out=tb_, in0=blka, scalar1=PB,
                                                         scalar2=CP, op0=ALU.mult, op1=ALU.add))
                    ctx.chain(eng, eng.scalar_tensor_tensor(out=blka, in0=blka, scalar=P1,
                                                            in1=ta, op0=ALU.mult, op1=ALU.add))
                    ctx.chain(eng, eng.scalar_tensor_tensor(out=blkb, in0=blkb, scalar=P1,
                                                            in1=tb_, op0=ALU.mult, op1=ALU.add))
            if subs[-1] == NSUB - 1:
                ctx.inc(eng.engine_nop(), "sV", 1)
                ctx.mark(("vfin", i), "sV")

        vH0e(0)
        for i in range(NT):
            r = i % 2
            b = seq_of(i)
            if i <= 1:
                ctx.wait(eng, ("zw", NT - 1))
                ctx.wait(eng, ("zw", NT - 2))
                ctx.wait(eng, ("gz", NT - 1))
            for cyc in (1, 2, 3):
                if cyc == 2:
                    ctx.wait(eng, ("cbe", i, 1))
                    ctx.wait(eng, ("eH0", i, KD - 1))
                    if i >= 2:
                        ctx.wait(eng, ("gel1", i - 2, 3, KD - 1))
                    for k in range(KD):
                        last = eng.tensor_mul(out=Gcur[r][:, k * TT:(k + 1) * TT],
                                              in0=C1b[r][:, :],
                                              in1=H0[r][:, k * TT:(k + 1) * TT])
                    ctx.chain(eng, last)
                    for k in range(KD):
                        ctx.inc(eng.tensor_add(out=Gcur[r][:, k * TT:(k + 1) * TT],
                                               in0=Gcur[r][:, k * TT:(k + 1) * TT],
                                               in1=G0[r][:, k * TT:(k + 1) * TT]),
                                "sV", 1)
                        ctx.mark(("gck", i, 2, k), "sV")
                    ctx.mark(("gc", i, 2), "sV")
                    if i >= 1:
                        vFinal(i - 1, (2, 3))
                if cyc == 3:
                    c1r = cf[r][0:1, 0:512]
                    c2r = cf[r][0:1, 512:1024]
                    p2r = cf[r][0:1, 1536:2048]
                    q2r = cf[r][0:1, 2048:2560]
                    ctx.wait(eng, ("cfm", i, 2))
                    ctx.chain(eng, eng.tensor_mul(out=p2r, in0=c1r, in1=c2r))
                    ctx.chain(eng, eng.tensor_scalar(out=p2r, in0=p2r, scalar1=-1.0,
                                                     scalar2=1.0, op0=ALU.mult, op1=ALU.add))
                    ctx.inc(eng.tensor_add(out=q2r, in0=c1r, in1=c2r), "sV", 1)
                    ctx.mark(("p2q2", i), "sV")
                    ctx.wait(eng, ("cbe", i, 3))
                    ctx.wait(eng, ("th2", i, 2, 3))
                    for k in range(KD):
                        eng.tensor_mul(out=Gcur[r][:, k * TT:(k + 1) * TT],
                                       in0=P2b[r][:, :], in1=G0[r][:, k * TT:(k + 1) * TT])
                    for k in range(KD):
                        last = eng.tensor_mul(out=h1t[r][:, k * TT:(k + 1) * TT],
                                              in0=Q2b[r][:, :],
                                              in1=H0[r][:, k * TT:(k + 1) * TT])
                    ctx.chain(eng, last)
                    for k in range(KD):
                        ctx.inc(eng.tensor_add(out=Gcur[r][:, k * TT:(k + 1) * TT],
                                               in0=Gcur[r][:, k * TT:(k + 1) * TT],
                                               in1=h1t[r][:, k * TT:(k + 1) * TT]),
                                "sV", 1)
                        ctx.mark(("gck", i, 3, k), "sV")
                    ctx.mark(("gc", i, 3), "sV")
                    if i + 1 < NT:
                        vH0e(i + 1)
                ctx.wait(eng, ("t4", i, cyc))
                ctx.wait(eng, ("a2",))
                if cyc == 1 and i >= 2:
                    ctx.wait(eng, ("ctT", i - 2))
                    ctx.wait(eng, ("tcb", i - 2, 3))
                ccol = cf[r][0:1, (cyc - 1) * 512:cyc * 512]
                ctx.inc(eng.tensor_scalar(
                    out=ccol, in0=t4b[r][0:1, :], scalar1=Bc[0:1, 0:1],
                    scalar2=A2[0:1, b:b + 1], op0=ALU.mult, op1=ALU.add), "sV", 1)
                ctx.mark(("cfm", i, cyc), "sV")

            ctx.wait(eng, ("ctT", i))
            ctx.inc(eng.tensor_copy(out=ctv[r][:, :], in_=pbb6[:, 0:24:2]), "sV", 1)
            ctx.mark(("ctv", i), "sV")
            ctx.wait(eng, ("ctv", i))
            cc1, cc2, cc3 = ctv[r][:, 0:4], ctv[r][:, 4:8], ctv[r][:, 8:12]
            o = cco[r]
            q12, q13, q23 = o[:, 0:4], o[:, 4:8], o[:, 8:12]
            a3, b3v = o[:, 12:16], o[:, 16:20]
            tmp = o[:, 20:24]
            ctx.chain(eng, eng.tensor_mul(out=q12, in0=cc1, in1=cc2))
            eng.tensor_mul(out=q13, in0=cc1, in1=cc3)
            ctx.chain(eng, eng.tensor_mul(out=q23, in0=cc2, in1=cc3))
            ctx.chain(eng, eng.tensor_add(out=a3, in0=q12, in1=q13))
            ctx.chain(eng, eng.tensor_add(out=a3, in0=a3, in1=q23))
            ctx.chain(eng, eng.tensor_scalar(out=a3, in0=a3, scalar1=-1.0, scalar2=1.0,
                                             op0=ALU.mult, op1=ALU.add))
            ctx.chain(eng, eng.tensor_mul(out=tmp, in0=q12, in1=cc3))
            ctx.chain(eng, eng.tensor_add(out=b3v, in0=cc1, in1=cc2))
            ctx.chain(eng, eng.tensor_add(out=b3v, in0=b3v, in1=cc3))
            ctx.chain(eng, eng.tensor_sub(out=b3v, in0=b3v, in1=tmp))
            vFinal(i, (0, 1))
        vFinal(NT - 1, (2, 3))

    def emit_scalar(eng):
        def sqp(i):
            ctx.wait(eng, ("s1v", i))
            ctx.inc(eng.activation(out=ssq[:, 4 * i:4 * i + 4],
                                   in_=svar[:, 4 * i:4 * i + 4], func=AF.Sqrt,
                                   bias=0.0, scale=1.0), "sS", 1)
            ctx.mark(("sqp", i), "sS")

        def sqz(i):
            ctx.wait(eng, ("s2v", i))
            ctx.inc(eng.activation(out=ssq[:, 4 * i:4 * i + 4],
                                   in_=svar[:, 4 * i:4 * i + 4], func=AF.Sqrt,
                                   bias=0.0, scale=1.0), "sS", 1)
            ctx.mark(("sqz", i), "sS")

        def gacts(b):
            for m in range(4):
                ctx.wait(eng, ("gc1", b, m))
                ctx.inc(eng.activation(out=hg1[:, b * 4 + m:b * 4 + m + 1],
                                       in_=pb[6][:, 64 + b * 4 + m:65 + b * 4 + m],
                                       func=AF.Gelu,
                                       bias=gb1c[:, m:m + 1], scale=1.0), "sS", 1)
            ctx.mark(("hg1", b), "sS")
            for m in range(2):
                ctx.wait(eng, ("gc2", b, m))
                ctx.inc(eng.activation(out=hg2[:, b * 2 + m:b * 2 + m + 1],
                                       in_=pb[6][:, 80 + b * 2 + m:81 + b * 2 + m],
                                       func=AF.Gelu, bias=gb2c[:, m:m + 1], scale=1.0),
                        "sS", 1)
            ctx.mark(("hg2", b), "sS")
            ctx.wait(eng, ("gc3", b))
            ctx.inc(eng.activation(out=tg[0:1, b:b + 1], in_=pb[6][0:1, 96 + b:97 + b],
                                   func=AF.Tanh, bias=gb3h[0:1, 1:2], scale=0.5), "sS", 1)
            ctx.mark(("tg", b), "sS")

        def spillx(i):
            sl = "sLN%d" % (i % 2)
            sb = "sLB%d" % (i % 2)
            src3 = xlnbf[i % 2].rearrange("p (j d) -> p j d", d=Dm)
            ctx.wait(eng, ("vln", i))
            ctx.epoch(eng, sl)
            ctx.inc(eng.dma_start(
                out=xln_scr[tok0(i):tok0(i) + TT, 0:512].rearrange(
                    "(j p) d -> p j d", p=128),
                in_=src3[:, :, 0:512]), sl, 16)
            ctx.mark(("lnwh", i), sl)
            ctx.epoch(eng, sb)
            ctx.inc(eng.dma_start(
                out=xln_scr[tok0(i):tok0(i) + TT, 512:1024].rearrange(
                    "(j p) d -> p j d", p=128),
                in_=src3[:, :, 512:1024]), sb, 16)
            ctx.mark(("lnw", i), sb)

        def spillz(i):
            sb = "sLB%d" % (i % 2)
            ctx.wait(eng, ("vz", i))
            ctx.epoch(eng, sb)
            ctx.inc(eng.dma_start(
                out=z_scr[tok0(i):tok0(i) + TT, :].rearrange(
                    "(j p) d -> p j d", p=128),
                in_=zsl[i % 2].rearrange("p (j d) -> p j d", d=Dm)), sb, 16)
            ctx.mark(("zw", i), sb)

        sqp(0)
        spillx(0)
        sqp(1)
        spillx(1)
        ctx.wait(eng, ("bias2",))

        for i in range(NT):
            if i + 2 < NT:
                sqp(i + 2)
            for m in range(KD):
                ctx.wait(eng, ("ta", i, m))
                if i >= 1:
                    ctx.wait(eng, ("vsc", i - 1, m))
                ctx.inc(eng.activation(out=alf[:, m * TT:(m + 1) * TT],
                                       in_=pb[m % 2][:, :], func=AF.Tanh,
                                       bias=ba2[:, m:m + 1], scale=0.5), "sS", 1)
                ctx.mark(("sa", i, m), "sS")
            if i >= 1:
                sqz(i - 1)
                spillz(i - 1)
            if i + 2 < NT:
                spillx(i + 2)
            if i == 6:
                gacts(0)
        sqz(NT - 1)
        spillz(NT - 1)
        gacts(1)
        def sG0e(i):
            r = i % 2
            if i == 1:
                ctx.wait(eng, ("zw", NT - 1))
                ctx.wait(eng, ("zw", NT - 2))
                ctx.wait(eng, ("gz", NT - 1))
            if i >= 2:
                ctx.wait(eng, ("gc", i - 2, 3))
                ctx.wait(eng, ("gel1", i - 2, 3, KD - 1))
            for g in range(KD):
                ctx.wait(eng, ("tG0", i, g))
                ctx.inc(eng.copy(out=G0[r][:, g * TT:(g + 1) * TT], in_=pb[g % 2][:, :]),
                        "sS", 1)
                ctx.mark(("eG0", i, g), "sS")

        def sCyc(i, cyc):
            r = i % 2
            if cyc == 1 and i >= 2:
                ctx.wait(eng, ("th2", i - 2, 3, 3))
                ctx.wait(eng, ("th3", i - 2, 3, 1))
                ctx.wait(eng, ("ttf", i - 2, 3))
                ctx.wait(eng, ("cfm", i - 2, 3))
            if cyc == 3:
                ctx.wait(eng, ("tcb", i, 3))
                eng.copy(out=P2b[r][:, :], in_=pb[3][:, :])
                ctx.inc(eng.copy(out=Q2b[r][:, :], in_=pb[5][:, :]), "sS", 1)
                ctx.mark(("cbe", i, 3), "sS")
            src_ = G0[r] if cyc == 1 else Gcur[r]
            if i <= 1 and cyc == 1:
                ctx.wait(eng, ("zw", NT - 1))
                ctx.wait(eng, ("zw", NT - 2))
                ctx.wait(eng, ("gz", NT - 1))
            if cyc == 1:
                ctx.wait(eng, ("eG0", i, KD - 1))
            for k in range(KD):
                if cyc > 1:
                    ctx.wait(eng, ("gck", i, cyc, k))
                ctx.inc(eng.activation(out=h1t[r][:, k * TT:(k + 1) * TT],
                                       in_=src_[:, k * TT:(k + 1) * TT], func=AF.Gelu,
                                       bias=b1c[:, k:k + 1], scale=1.0), "sS", 1)
                ctx.mark(("gel1", i, cyc, k), "sS")
            for m in range(4):
                ctx.wait(eng, ("th2", i, cyc, m))
                ctx.inc(eng.activation(out=h2t[r][:, m * TT:(m + 1) * TT],
                                       in_=pb[m % 2][:, :], func=AF.Gelu,
                                       bias=b2c[:, m:m + 1], scale=1.0), "sS", 1)
                ctx.mark(("gel2", i, cyc, m), "sS")
            for m in range(2):
                ctx.wait(eng, ("th3", i, cyc, m))
                ctx.inc(eng.activation(out=h3t[r][:, m * TT:(m + 1) * TT],
                                       in_=pb[m % 2][:, :], func=AF.Gelu,
                                       bias=b3c[:, m:m + 1], scale=1.0), "sS", 1)
                ctx.mark(("gel3", i, cyc, m), "sS")
            ctx.wait(eng, ("ttf", i, cyc))
            if cyc > 1:
                ctx.wait(eng, ("cfm", i, cyc - 1))
            elif i > 0:
                ctx.wait(eng, ("cfm", i - 1, 3))
            ctx.inc(eng.activation(out=t4b[r][0:1, :], in_=pb[2][0:1, :],
                                   func=AF.Tanh, bias=b4h[0:1, 1:2], scale=0.5), "sS", 1)
            ctx.mark(("t4", i, cyc), "sS")
            if cyc == 1:
                ctx.wait(eng, ("tcb", i, 1))
                ctx.inc(eng.copy(out=C1b[r][:, :], in_=pb[3][:, :]), "sS", 1)
                ctx.mark(("cbe", i, 1), "sS")

        for i in range(NT):
            if i >= 1:
                sCyc(i - 1, 1)
            sG0e(i)
            if i >= 1:
                sCyc(i - 1, 2)
                sCyc(i - 1, 3)
        sCyc(NT - 1, 1)
        sCyc(NT - 1, 2)
        sCyc(NT - 1, 3)
    def emit_tensor(eng):
        ctx.wait(eng, ("wattn1",))

        def zmean(zi):
            b = seq_of(zi)
            first = (zi % TPS == 0)
            last_t = (zi % TPS == TPS - 1)
            if zi == TPS:
                ctx.wait(eng, ("tg", 0))
            ctx.wait(eng, ("vz", zi))
            for s in range(NSUB):
                for h in range(2):
                    mm = eng.matmul(pb[6 + h][0:1, :], onesc_b[:, :],
                                    zsl[zi % 2][:, s * Dm + h * 512:s * Dm + (h + 1) * 512],
                                    start=(first and s == 0),
                                    stop=(last_t and s == NSUB - 1),
                                    skip_group_check=True)
            ctx.inc(mm, "sT", 1)
            ctx.mark(("gz", zi), "sT")
            if last_t:
                ctx.mark(("gacc", b), "sT")

        def gfmT(b):
            ctx.wait(eng, ("gsb", b))
            for k in range(KD):
                c_ = 2 * (b * 8 + k)
                mm = eng.matmul(pbb6[:, c_:c_ + 1],
                                gsb[0:1, b * Dm + k * 128:b * Dm + (k + 1) * 128],
                                id1b[0:1, 0:1], start=(k == 0), stop=(k == KD - 1),
                                is_transpose=True, skip_group_check=True)
            ctx.inc(mm, "sT", 1)
            ctx.mark(("gfmT", b), "sT")

        def gcpart(b):
            ctx.wait(eng, ("wffn3",))
            ctx.wait(eng, ("gfmb", b))
            for m in range(4):
                for k in range(KD):
                    mm = eng.matmul(pb[6][:, 64 + b * 4 + m:65 + b * 4 + m],
                                    g1[k][:, m * 128:(m + 1) * 128],
                                    gfmb[:, b * 8 + k:b * 8 + k + 1],
                                    start=False, stop=(k == KD - 1),
                                    skip_group_check=True)
                ctx.inc(mm, "sT", 1)
                ctx.mark(("gc1", b, m), "sT")
            ctx.wait(eng, ("hg1", b))
            for m in range(2):
                for k in range(4):
                    mm = eng.matmul(pb[6][:, 80 + b * 2 + m:81 + b * 2 + m],
                                    g2[k][:, m * 128:(m + 1) * 128],
                                    hg1[:, b * 4 + k:b * 4 + k + 1],
                                    start=False, stop=(k == 3),
                                    skip_group_check=True)
                ctx.inc(mm, "sT", 1)
                ctx.mark(("gc2", b, m), "sT")
            ctx.wait(eng, ("hg2", b))
            for k in range(2):
                mm = eng.matmul(pb[6][0:1, 96 + b:97 + b], g3[k][0:128, 0:1],
                                hg2[:, b * 2 + k:b * 2 + k + 1],
                                start=False, stop=(k == 1), skip_group_check=True)
            ctx.inc(mm, "sT", 1)
            ctx.mark(("gc3", b), "sT")

        def yblk(yi):
            ctx.wait(eng, ("vhs", yi))
            if yi == 0:
                ctx.wait(eng, ("dstrl",))
                ctx.wait(eng, ("wattn3",))
            for s in range(NSUB):
                for h in range(2):
                    g = 2 * s + h
                    if yi > 0 or g >= 2:
                        pi, pg = (yi, g - 2) if g >= 2 else (yi - 1, g + 6)
                        ctx.wait(eng, ("vx1g", pi, pg))
                    for k in range(KD):
                        eng.matmul(
                            pb[4 + g % 2][:, :],
                            hsbf[yi % 2][:, k * TT + s * 128:k * TT + (s + 1) * 128],
                            cT[k][:, h * 512:(h + 1) * 512],
                            start=(k == 0), stop=False)
                    for kr in range(4):
                        kdm = 4 * h + kr
                        mm = eng.matmul(
                            pb[4 + g % 2][:, :],
                            xfm[yi % 2][:, kdm * TT + s * 128:kdm * TT + (s + 1) * 128],
                            dstrb[:, (h * 4 + kr) * 512:(h * 4 + kr + 1) * 512],
                            start=False, stop=(kr == 3))
                    ctx.inc(mm, "sT", 1)
                    ctx.mark(("ty", yi, s, h), "sT")
            ctx.mark(("tyall", yi), "sT")

        for i in range(NT):
            ctx.wait(eng, ("tr", i))
            for m in range(KD):
                if i > 0 or m >= 2:
                    pi, pm = (i, m - 2) if m >= 2 else (i - 1, m + 6)
                    ctx.wait(eng, ("sa", pi, pm))
                for k in range(KD):
                    mm = eng.matmul(pb[m % 2][:, :], waT[k][:, m * 128:(m + 1) * 128],
                                    xfm[i % 2][:, k * TT:(k + 1) * TT],
                                    start=(k == 0), stop=(k == KD - 1))
                ctx.inc(mm, "sT", 1)
                ctx.mark(("ta", i, m), "sT")
            if 2 <= i <= 5:
                zmean(i - 2)
            if i == 7:
                zmean(4)
                zmean(5)
            if i == 6:
                gfmT(0)
            if i > 0:
                yblk(i - 1)
            if i == 0:
                ctx.wait(eng, ("wattn2",))
                ctx.wait(eng, ("smalls",))
                ctx.wait(eng, ("consts",))
            for m in range(KD):
                if i > 0 or m >= 2:
                    pi, pm = (i, m - 2) if m >= 2 else (i - 1, m + 6)
                    ctx.wait(eng, ("vsc", pi, pm))
                for k in range(KD):
                    eng.matmul(pb[2 + m % 2][:, :], wbT[k][:, m * 128:(m + 1) * 128],
                               xfm[i % 2][:, k * TT:(k + 1) * TT],
                               start=(k == 0), stop=False)
                mm = eng.matmul(pb[2 + m % 2][:, :], bbrow[0:1, m * 128:(m + 1) * 128],
                                ones_b1[0:1, :], start=False, stop=True)
                ctx.inc(mm, "sT", 1)
                ctx.mark(("tb", i, m), "sT")
            ctx.mark(("tmma", i), "sT")
            if i == 6:
                gcpart(0)
            if i == NT - 1:
                yblk(i)
                zmean(6)
                zmean(7)
                gfmT(1)
                gcpart(1)

        bankq = [None, None]

        def bigmm(consumer_mark, emitfn):
            j = bigmm.idx
            bigmm.idx += 1
            bk = j % 2
            if bankq[bk] is not None:
                ctx.wait(eng, bankq[bk])
            emitfn(pb[bk])
            bankq[bk] = consumer_mark
        bigmm.idx = 0

        ctx.wait(eng, ("w1p",))
        ctx.wait(eng, ("wffn1",))
        ctx.wait(eng, ("wffn2",))
        def tG0(i):
            r = i % 2
            ctx.wait(eng, ("ztr", i))
            for g in range(KD):
                def f(bank, g=g, i=i, r=r):
                    for k in range(KD):
                        mm = eng.matmul(bank[:, :], w1[k][:, g * 128:(g + 1) * 128],
                                        zfm[r][:, k * TT:(k + 1) * TT],
                                        start=(k == 0), stop=(k == KD - 1))
                    ctx.inc(mm, "sT", 1)
                    ctx.mark(("tG0", i, g), "sT")
                bigmm(("eG0", i, g), f)

        def tH0(i):
            r = i % 2
            for g in range(KD):
                def f(bank, g=g, i=i, r=r):
                    for k in range(KD):
                        mm = eng.matmul(bank[:, :], w1p[k][:, g * 128:(g + 1) * 128],
                                        zfm[r][:, k * TT:(k + 1) * TT],
                                        start=(k == 0), stop=(k == KD - 1))
                    ctx.inc(mm, "sT", 1)
                    ctx.mark(("tH0", i, g), "sT")
                bigmm(("eH0", i, g), f)
            ctx.mark(("tg0h0", i), "sT")

        def tCyc(i, cyc):
            r = i % 2
            if True:
                for m in range(4):
                    def f(bank, m=m, i=i, r=r, cyc=cyc):
                        for k in range(KD):
                            if m == 0:
                                ctx.wait(eng, ("gel1", i, cyc, k))
                            mm = eng.matmul(bank[:, :], w2[k][:, m * 128:(m + 1) * 128],
                                            h1t[r][:, k * TT:(k + 1) * TT],
                                            start=(k == 0), stop=(k == KD - 1))
                        ctx.inc(mm, "sT", 1)
                        ctx.mark(("th2", i, cyc, m), "sT")
                    bigmm(("gel2", i, cyc, m), f)
                for m in range(2):
                    def f(bank, m=m, i=i, r=r, cyc=cyc):
                        for k in range(4):
                            if m == 0:
                                ctx.wait(eng, ("gel2", i, cyc, k))
                            mm = eng.matmul(bank[:, :], w3[k][:, m * 128:(m + 1) * 128],
                                            h2t[r][:, k * TT:(k + 1) * TT],
                                            start=(k == 0), stop=(k == 3))
                        ctx.inc(mm, "sT", 1)
                        ctx.mark(("th3", i, cyc, m), "sT")
                    bigmm(("gel3", i, cyc, m), f)
                if i == 0 and cyc == 1:
                    ctx.wait(eng, ("tg", 0))
                    ctx.wait(eng, ("tg", 1))
                else:
                    ctx.wait(eng, ("t4", i, cyc - 1) if cyc > 1 else ("t4", i - 1, 3))
                for k in range(2):
                    ctx.wait(eng, ("gel3", i, cyc, k))
                    mm = eng.matmul(pb[2][0:1, :], w4[k][0:128, 0:1],
                                    h3t[r][:, k * TT:(k + 1) * TT],
                                    start=(k == 0), stop=(k == 1))
                ctx.inc(mm, "sT", 1)
                ctx.mark(("ttf", i, cyc), "sT")
                if cyc == 1:
                    ctx.wait(eng, ("cfm", i, 1))
                    if i > 0:
                        ctx.wait(eng, ("cbe", i - 1, 3))
                    ctx.inc(eng.matmul(pb[3][:, :], ones_bf[:, :], cf[r][0:1, 0:512],
                                       start=True, stop=True), "sT", 1)
                    ctx.mark(("tcb", i, 1), "sT")
                ctx.wait(eng, ("cfm", i, cyc))
                if cyc == 1 and i >= 1:
                    ctx.wait(eng, ("ctv", i - 1))
                for s in range(NSUB):
                    c_ = 2 * (4 * (cyc - 1) + s)
                    mm = eng.matmul(
                        pbb6[:, c_:c_ + 1],
                        cf[r][0:1, (cyc - 1) * 512 + s * 128:(cyc - 1) * 512 + (s + 1) * 128],
                        id1b[0:1, 0:1],
                        start=(cyc == 1 and s == 0), stop=(cyc == 3 and s == 3),
                        is_transpose=True, skip_group_check=True)
                if cyc == 3:
                    ctx.inc(mm, "sT", 1)
                    ctx.mark(("ctT", i), "sT")
                if cyc == 2:
                    ctx.wait(eng, ("p2q2", i))
                    ctx.wait(eng, ("cbe", i, 1))
                    if i > 0:
                        ctx.wait(eng, ("cbe", i - 1, 3))
                    eng.matmul(pb[3][:, :], ones_bf[:, :], cf[r][0:1, 1536:2048],
                               start=True, stop=True)
                    ctx.inc(eng.matmul(pb[5][:, :], ones_bf[:, :], cf[r][0:1, 2048:2560],
                                       start=True, stop=True), "sT", 1)
                    ctx.mark(("tcb", i, 3), "sT")


        for i in range(NT):
            if i >= 1:
                tCyc(i - 1, 1)
            tG0(i)
            if i >= 1:
                tCyc(i - 1, 2)
            tH0(i)
            if i >= 1:
                tCyc(i - 1, 3)
        tCyc(NT - 1, 1)
        tCyc(NT - 1, 2)
        tCyc(NT - 1, 3)
    ctx.reset(False)
    dummy = _DummyEng()
    emit_gpsimd(dummy)
    emit_sync(dummy)
    emit_vector(dummy)
    emit_scalar(dummy)
    emit_tensor(dummy)

    import contextlib
    with contextlib.ExitStack() as stack:
        for s in Ctx.SEMS:
            ctx.handles[s] = stack.enter_context(nc.semaphore(s))
        block = stack.enter_context(nc.Block())
        ctx.reset(True)
        block.gpsimd(lambda e: emit_gpsimd(e))
        block.sync(lambda e: emit_sync(e))
        block.vector(lambda e: emit_vector(e))
        block.scalar(lambda e: emit_scalar(e))
        block.tensor(lambda e: emit_tensor(e))
    return nc


def kernel(**inputs):
    u = np.ascontiguousarray(inputs["L0_attn_u"])

    def chunked(v, nk):
        return np.ascontiguousarray(np.asarray(v).reshape(nk, 128).T)

    def dstrips(D_):
        ds = np.zeros((8, 128, 512), np.float32)
        idx = np.arange(128)
        for h in range(2):
            for kr in range(4):
                ds[h * 4 + kr, idx, kr * 128 + idx] = D_[(4 * h + kr) * 128:
                                                         (4 * h + kr + 1) * 128]
        return ds

    shared = {
        "waT": np.ascontiguousarray(np.asarray(inputs["L0_attn_W_alpha_w"]).T),
        "wbT": np.ascontiguousarray(np.asarray(inputs["L0_attn_W_B_w"]).T),
        "cT": np.ascontiguousarray(np.asarray(inputs["L0_attn_C_w"]).T),
        "w1": np.ascontiguousarray(inputs["L0_ffn_cn_w1"]),
        "w2": np.ascontiguousarray(inputs["L0_ffn_cn_w2"]),
        "w3": np.ascontiguousarray(inputs["L0_ffn_cn_w3"]),
        "w4": np.ascontiguousarray(inputs["L0_ffn_cn_w4"]),
        "g1": np.ascontiguousarray(inputs["L0_ffn_gc_w1"]),
        "g2": np.ascontiguousarray(inputs["L0_ffn_gc_w2"]),
        "g3": np.ascontiguousarray(inputs["L0_ffn_gc_w3"]),
        "ba": chunked(inputs["L0_attn_W_alpha_b"], 8),
        "bbrow_d": np.ascontiguousarray(np.asarray(inputs["L0_attn_W_B_b"]).reshape(1, Dm)),
        "b1": chunked(inputs["L0_ffn_cn_b1"], 8),
        "b2": chunked(inputs["L0_ffn_cn_b2"], 4),
        "b3": chunked(inputs["L0_ffn_cn_b3"], 2),
        "b4": np.ascontiguousarray(np.asarray(inputs["L0_ffn_cn_b4"]).reshape(1, 1)),
        "gb1": chunked(inputs["L0_ffn_gc_b1"], 4),
        "gb2": chunked(inputs["L0_ffn_gc_b2"], 2),
        "gb3": np.ascontiguousarray(np.asarray(inputs["L0_ffn_gc_b3"]).reshape(1, 1)),
        "ar": np.ascontiguousarray(np.asarray(inputs["L0_ffn_adaptive_range"]).reshape(1, 1)),
        "dstr": dstrips(np.asarray(inputs["L0_attn_D"]).reshape(Dm)),
    }
    nc = build_program()
    in_maps = []
    for c in range(NCORES):
        m = dict(shared)
        m["u"] = np.ascontiguousarray(u[SEQ * c:SEQ * (c + 1)])
        in_maps.append(m)
    res = run_bass_kernel_spmd(nc, in_maps, core_ids=list(range(NCORES)))
    return np.concatenate([np.asarray(r["out"]) for r in res.results], axis=0)


if __name__ == "__main__":
    build_program()
    print("built OK")


# version 21
# speedup vs baseline: 1.0102x; 1.0037x over previous
import numpy as np
import concourse.bass as bass
import concourse.mybir as mybir
from concourse.bass_utils import run_bass_kernel_spmd

F32 = mybir.dt.float32
BF16 = mybir.dt.bfloat16
AF = mybir.ActivationFunctionType
ALU = mybir.AluOpType
AXX = mybir.AxisListType.X

B, T, Dm = 16, 2048, 1024
NCORES = 8
SEQ = B // NCORES
TPC = SEQ * T
TT = 512
NT = TPC // TT
TPS = NT // SEQ
NSUB = TT // 128
KD = Dm // 128
NSUBS = NT * NSUB
EPS = 1e-5


class _DummyInstr:
    def then_inc(self, *a, **k):
        return self


class _DummyEng:
    def __getattr__(self, name):
        def f(*a, **kw):
            return _DummyInstr()
        return f


class Ctx:
    SEMS = ["sW1", "sW2", "sW3", "sGs", "sX0", "sX1", "sLN0", "sLN1",
            "sLB0", "sLB1", "sYt0", "sYt1", "sYs0", "sYs1", "sOUT",
            "sT", "sV", "sS", "sVt", "sTS0", "sTS1", "sXA0", "sXA1"]

    def __init__(self):
        self.emit = False
        self.handles = {}
        self.cnt = {s: 0 for s in self.SEMS}
        self.marks = {}

    def reset(self, emit):
        self.emit = emit
        self.cnt = {s: 0 for s in self.SEMS}

    def inc(self, instr, sem, n=1):
        self.cnt[sem] += n
        if self.emit:
            instr.then_inc(self.handles[sem], n)
        return instr

    def mark(self, key, sem):
        self.marks[key] = (sem, self.cnt[sem])

    def wait(self, eng, key):
        sem, val = self.marks.get(key, (None, 0))
        if self.emit and val > 0:
            eng.wait_ge(self.handles[sem], val)

    def chain(self, eng, instr):
        self.inc(instr, "sVt", 1)
        if self.emit:
            eng.wait_ge(self.handles["sVt"], self.cnt["sVt"])

    def epoch(self, eng, sem):
        if self.emit and self.cnt[sem] > 0:
            eng.wait_ge(self.handles[sem], self.cnt[sem])


def build_program():
    nc = bass.Bass("TRN2", target_bir_lowering=False)

    u = nc.dram_tensor("u", [SEQ, T, Dm], F32, kind="ExternalInput").ap()
    DI = lambda n, s: nc.dram_tensor(n, list(s), F32, kind="ExternalInput").ap()
    waT_d = DI("waT", (Dm, Dm))
    wbT_d = DI("wbT", (Dm, Dm))
    cT_d = DI("cT", (Dm, Dm))
    w1_d = DI("w1", (Dm, Dm))
    w2_d = DI("w2", (Dm, 512))
    w3_d = DI("w3", (512, 256))
    w4_d = DI("w4", (256, 1))
    g1_d = DI("g1", (Dm, 512))
    g2_d = DI("g2", (512, 256))
    g3_d = DI("g3", (256, 1))
    ba_d = DI("ba", (128, 8))
    bbrow_d = DI("bbrow_d", (1, Dm))
    b1_d = DI("b1", (128, 8))
    b2_d = DI("b2", (128, 4))
    b3_d = DI("b3", (128, 2))
    b4_d = DI("b4", (1, 1))
    gb1_d = DI("gb1", (128, 4))
    gb2_d = DI("gb2", (128, 2))
    gb3_d = DI("gb3", (1, 1))
    ar_d = DI("ar", (1, 1))
    dstr_d = DI("dstr", (8, 128, 512))
    out = nc.dram_tensor("out", [SEQ, T, Dm], F32, kind="ExternalOutput").ap()

    xln_scr = nc.dram_tensor("xln_scr", [TPC, Dm], BF16, kind="Internal").ap()
    x1_scr = nc.dram_tensor("x1_scr", [TPC, Dm], F32, kind="Internal").ap()
    z_scr = nc.dram_tensor("z_scr", [TPC, Dm], BF16, kind="Internal").ap()

    A = lambda name, shape, dt, **kw: nc.alloc_sbuf_tensor(name, shape, dt, **kw).ap()

    WB = A("WB", [128, 27008], BF16)
    waT = [WB[:, k * 1024:(k + 1) * 1024] for k in range(KD)]
    wbT = [WB[:, 8192 + k * 1024:8192 + (k + 1) * 1024] for k in range(KD)]
    cT = [WB[:, 16384 + k * 1024:16384 + (k + 1) * 1024] for k in range(KD)]
    w1 = [WB[:, k * 1024:(k + 1) * 1024] for k in range(KD)]
    w1p = [WB[:, 8192 + k * 1024:8192 + (k + 1) * 1024] for k in range(KD)]
    w2 = [WB[:, 16384 + k * 512:16384 + (k + 1) * 512] for k in range(KD)]
    w3 = [WB[:, 20480 + k * 256:20480 + (k + 1) * 256] for k in range(4)]
    w4 = [WB[:, 21504 + k:21504 + k + 1] for k in range(2)]


    dstrb = A("dstrb", [128, KD * 512], BF16)
    ba2 = A("ba2", [128, 8], F32)
    bac = A("bac", [128, 8], F32)
    b1c = A("b1c", [128, 8], F32)
    b2c = A("b2c", [128, 4], F32)
    b3c = A("b3c", [128, 2], F32)
    b4h = A("b4h", [1, 2], F32)
    gb1c = A("gb1c", [128, 4], F32)
    gb2c = A("gb2c", [128, 2], F32)
    gb3h = A("gb3h", [1, 2], F32)
    arv = A("arv", [1, 1], F32)
    bbrow = A("bbrow", [1, Dm], BF16)
    ones_f = A("ones_f", [1, 128], F32)
    ones_bf = A("ones_bf", [1, 128], BF16)
    ones_b1 = A("ones_b1", [1, TT], BF16)
    onesc_b = A("onesc_b", [128, 1], BF16)
    id1f = A("id1f", [1, 1], F32)
    id1b = A("id1b", [1, 1], BF16)

    bnraw = A("bnraw", [128, NSUBS * 12], F32)
    mean1 = A("mean1", [128, NSUBS], F32)
    r1 = A("r1", [128, NSUBS], F32)
    mean2 = A("mean2", [128, NSUBS], F32)
    r2t = A("r2t", [128, NSUBS], F32)
    svar = A("svar", [128, NSUBS], F32)
    ssq = A("ssq", [128, NSUBS], F32)
    st1 = A("st1", [128, NSUBS], F32)
    st2 = A("st2", [128, NSUBS], F32)
    stsq = A("stsq", [128, NSUBS * 4], F32)
    carry = A("carry", [128, 8], F32)

    xb = [A("xb0", [128, NSUB * Dm], F32), A("xb1", [128, NSUB * Dm], F32)]
    xlnbf = [A("xlnbf0", [128, NSUB * Dm], BF16), A("xlnbf1", [128, NSUB * Dm], BF16)]
    xfm = [A("xfm0", [128, KD * TT], BF16, is_transpose_dest=True),
           A("xfm1", [128, KD * TT], BF16, is_transpose_dest=True)]
    alf = A("alf", [128, KD * TT], F32)
    hsf = A("hsf", [128, KD * TT], F32)
    hsbf = [A("hsbf0", [128, KD * TT], BF16), A("hsbf1", [128, KD * TT], BF16)]
    bex = A("bex", [128, 4096], F32)
    bex2 = A("bex2", [128, 640], F32)
    fsc = A("fsc", [128, 512], F32)
    cvf = A("cvf", [128, 5], F32)

    x1b = xb
    zbf = xlnbf
    zfm = xfm
    alf_b = alf.bitcast(BF16)
    hsf_b = hsf.bitcast(BF16)
    zsl = [hsf_b[:, 0:4096], hsf_b[:, 4096:8192]]
    G0 = [alf_b[:, 0:4096], hsf_b[:, 0:4096]]
    H0 = [alf_b[:, 4096:8192], hsf_b[:, 4096:8192]]
    Gcur = [xlnbf[0][:, 0:4096], hsbf[0][:, 0:4096]]
    h1t = [xlnbf[1][:, 0:4096], hsbf[1][:, 0:4096]]
    bex_b = bex.bitcast(BF16)
    g1 = [bex_b[:, k * 512:(k + 1) * 512] for k in range(KD)]
    g2 = [bex_b[:, 4096 + k * 256:4096 + (k + 1) * 256] for k in range(4)]
    g3 = [bex_b[:, 5120 + k:5120 + k + 1] for k in range(2)]
    h2t = [bex_b[:, 0:2048], bex_b[:, 2048:4096]]
    h3t = [bex_b[:, 4096:5120], bex_b[:, 5120:6144]]
    C1b = [bex_b[:, 6144:6656], bex_b[:, 6656:7168]]
    P2b = [bex_b[:, 7168:7680], bex_b[:, 7680:8192]]
    bex2_b = bex2.bitcast(BF16)
    Q2b = [bex2_b[:, 0:512], bex2_b[:, 512:1024]]
    gfmb = bex2_b[:, 1024:1040]
    hg1 = bex2_b[:, 1040:1048]
    hg2 = bex2_b[:, 1048:1052]
    ctv = [A("ctv0", [128, 12], F32), A("ctv1", [128, 12], F32)]
    cco = [A("cco0", [128, 24], F32), A("cco1", [128, 24], F32)]
    cf_ = A("cf0", [1, 2560], BF16)
    cf = [cf_, cf_]
    t4b_ = A("t4b", [1, TT], F32)
    t4b = [t4b_, t4b_]
    gsb = cf_[0:1, 0:2 * Dm]
    tg = st2[0:1, 0:2]
    A2 = st2[0:1, 2:4]
    Bc = st2[0:1, 4:5]

    pb = [nc.alloc_psum_tensor("pb%d" % i, [128, 512], F32).ap() for i in range(8)]
    pbb2 = pb[2].bitcast(BF16)
    pbb6 = pb[6].bitcast(BF16)

    assert nc.sbuf_bytes_remaining >= 0

    ctx = Ctx()
    _li = [_DummyInstr()]

    def seq_of(i):
        return i // TPS

    def t0_of(i):
        return (i % TPS) * TT

    def tok0(i):
        return i * TT

    def rearr_d(d2):
        return d2.rearrange("(j p) d -> p j d", p=128)

    def emit_gpsimd(eng):
        for dst, src in [(bac, ba_d), (b1c, b1_d), (b2c, b2_d), (b3c, b3_d),
                         (gb1c, gb1_d), (gb2c, gb2_d)]:
            ctx.inc(eng.dma_start(out=dst[:, :], in_=src), "sGs", 16)
        ctx.inc(eng.dma_start(out=b4h[0:1, 0:1], in_=b4_d), "sGs", 16)
        ctx.inc(eng.dma_start(out=gb3h[0:1, 0:1], in_=gb3_d), "sGs", 16)
        ctx.inc(eng.dma_start(out=arv[:, :], in_=ar_d), "sGs", 16)
        ctx.inc(eng.dma_start(out=bbrow[:, :], in_=bbrow_d), "sGs", 16)
        ctx.mark(("smalls",), "sGs")

        def plxload(i):
            sx = "sX%d" % (i % 2)
            if i >= 1:
                ctx.wait(eng, ("vln", i - 1))
            ctx.epoch(eng, sx)
            ctx.inc(eng.dma_start(
                out=bex.rearrange("p (j d) -> p j d", d=Dm),
                in_=rearr_d(u[seq_of(i), t0_of(i):t0_of(i) + TT, :])), sx, 16)
            ctx.mark(("plx", i), sx)

        plxload(0)
        ctx.inc(eng.dma_start(
            out=WB[:, 0:8192].rearrange("p (k c) -> p k c", c=1024),
            in_=waT_d.rearrange("(k p) c -> p k c", p=128)), "sW1", 16)
        ctx.mark(("wattn1",), "sW1")
        ctx.epoch(eng, "sGs")
        ctx.inc(eng.dma_start(out=dstrb.rearrange("p (q c) -> p q c", c=512),
                              in_=dstr_d.rearrange("q p c -> p q c")), "sGs", 16)
        ctx.mark(("dstrl",), "sGs")
        plxload(1)
        ctx.inc(eng.dma_start(
            out=WB[:, 8192:16384].rearrange("p (k c) -> p k c", c=1024),
            in_=wbT_d.rearrange("(k p) c -> p k c", p=128)), "sW2", 16)
        ctx.mark(("wattn2",), "sW2")
        plxload(2)
        ctx.inc(eng.dma_start(
            out=WB[:, 16384:24576].rearrange("p (k c) -> p k c", c=1024),
            in_=cT_d.rearrange("(k p) c -> p k c", p=128)), "sW3", 16)
        ctx.mark(("wattn3",), "sW3")
        for i in range(3, NT):
            plxload(i)
        ctx.wait(eng, ("vln", NT - 1))
        ctx.epoch(eng, "sW3")
        ctx.inc(eng.dma_start(
            out=bex_b[:, 0:4096].rearrange("p (k c) -> p k c", c=512),
            in_=g1_d.rearrange("(k p) c -> p k c", p=128)), "sW3", 16)
        ctx.inc(eng.dma_start(
            out=bex_b[:, 4096:5120].rearrange("p (k c) -> p k c", c=256),
            in_=g2_d.rearrange("(k p) c -> p k c", p=128)), "sW3", 16)
        for k in range(2):
            ctx.inc(eng.dma_start(out=g3[k], in_=g3_d[k * 128:(k + 1) * 128, :]), "sW3", 16)
        ctx.mark(("wffn3",), "sW3")

        ctx.wait(eng, ("ta", NT - 1, KD - 1))
        ctx.epoch(eng, "sW1")
        ctx.inc(eng.dma_start(
            out=WB[:, 0:8192].rearrange("p (k c) -> p k c", c=1024),
            in_=w1_d.rearrange("(k p) c -> p k c", p=128)), "sW1", 16)
        ctx.mark(("wffn1",), "sW1")
        ctx.wait(eng, ("tyall", NT - 1))
        ctx.epoch(eng, "sW2")
        ctx.epoch(eng, "sW3")
        ctx.inc(eng.dma_start(
            out=WB[:, 16384:20480].rearrange("p (k c) -> p k c", c=512),
            in_=w2_d.rearrange("(k p) c -> p k c", p=128)), "sW2", 16)
        ctx.inc(eng.dma_start(
            out=WB[:, 20480:21504].rearrange("p (k c) -> p k c", c=256),
            in_=w3_d.rearrange("(k p) c -> p k c", p=128)), "sW2", 16)
        for k in range(2):
            ctx.inc(eng.dma_start(out=w4[k], in_=w4_d[k * 128:(k + 1) * 128, :]), "sW2", 16)
        ctx.mark(("wffn2",), "sW2")

        for j in range(NT + 2):
            if j >= 2:
                i = j - 2
                ctx.wait(eng, ("vfin", i))
                ctx.epoch(eng, "sOUT")
                ctx.inc(eng.dma_start(
                    out=rearr_d(out[seq_of(i), t0_of(i):t0_of(i) + TT, :]),
                    in_=x1b[i % 2].rearrange("p (j d) -> p j d", d=Dm)), "sOUT", 16)
                ctx.mark(("outd", i), "sOUT")
            if j < NT:
                i = j
                sx = "sX%d" % (i % 2)
                if i < 2:
                    ctx.wait(eng, ("spall", NT - 1))
                    ctx.wait(eng, ("spall", NT - 2))
                    ctx.wait(eng, ("vz", NT - 2 + i))
                else:
                    ctx.wait(eng, ("outd", i - 2))
                ctx.epoch(eng, sx)
                ctx.inc(eng.dma_start(
                    out=x1b[i % 2].rearrange("p (j d) -> p j d", d=Dm),
                    in_=rearr_d(x1_scr[tok0(i):tok0(i) + TT, :])), sx, 16)
                ctx.mark(("b2x", i), sx)

    def emit_sync(eng):
        for j in range(NT + 2):
            if j < NT:
                i = j
                st_ = "sYt%d" % (i % 2)
                ctx.wait(eng, ("lnwh", i))
                ctx.wait(eng, ("lnw", i))
                if i >= 2:
                    ctx.wait(eng, ("tyall", i - 2))
                ctx.epoch(eng, st_)
                for k in range(KD):
                    ctx.inc(eng.dma_start_transpose(
                        out=xfm[i % 2][:, k * TT:(k + 1) * TT],
                        in_=xln_scr[tok0(i):tok0(i) + TT, k * 128:(k + 1) * 128]),
                        st_, 16)
                ctx.mark(("tr", i), st_)
            if j >= 2:
                i = j - 2
                ss_ = "sYs%d" % (i % 2)
                ctx.epoch(eng, ss_)
                for s in range(NSUB):
                    ctx.wait(eng, ("vx1s", i, s))
                    ctx.inc(eng.dma_start(
                        out=x1_scr[tok0(i) + s * 128:tok0(i) + (s + 1) * 128, :],
                        in_=xb[i % 2][:, s * Dm:(s + 1) * Dm]), ss_, 16)
                ctx.mark(("spall", i), ss_)
            if j < NT:
                i = j
                sx = "sXA%d" % (i % 2)
                if i >= 2:
                    ctx.wait(eng, ("spall", i - 2))
                    ctx.wait(eng, ("vz", i - 2))
                    ctx.wait(eng, ("lx", i - 2))
                ctx.inc(eng.dma_start(
                    out=xb[i % 2].rearrange("p (j d) -> p j d", d=Dm),
                    in_=rearr_d(u[seq_of(i), t0_of(i):t0_of(i) + TT, :])), sx, 16)
                ctx.mark(("lx", i), sx)

        for j in range(NT + 2):
            if j < NT:
                i = j
                st_ = "sYt%d" % (i % 2)
                ctx.wait(eng, ("zw", i))
                if i >= 2:
                    ctx.wait(eng, ("tg0h0", i - 2))
                else:
                    ctx.wait(eng, ("tyall", NT - 2 + i))
                ctx.epoch(eng, st_)
                for k in range(KD):
                    ctx.inc(eng.dma_start_transpose(
                        out=zfm[i % 2][:, k * TT:(k + 1) * TT],
                        in_=z_scr[tok0(i):tok0(i) + TT, k * 128:(k + 1) * 128]),
                        st_, 16)
                ctx.mark(("ztr", i), st_)
        if ctx.emit:
            eng.wait_ge(ctx.handles["sOUT"], ctx.cnt["sOUT"])
            for s_ in ("sLN0", "sLN1", "sYs0", "sYs1"):
                eng.wait_ge(ctx.handles[s_], ctx.cnt[s_])

    def ln_combine(eng, meanout, rout, tag, prior_key):
        ctx.wait(eng, prior_key)
        v3 = bnraw.rearrange("p (s v) -> p s v", v=12)
        mcols = v3[:, :, 1:12:3]
        vcols = v3[:, :, 2:12:3]
        sq3 = stsq.rearrange("p (s v) -> p s v", v=4)
        ctx.chain(eng, eng.reduce_sum(out=st1[:, :], in_=mcols, axis=AXX))
        ctx.chain(eng, eng.tensor_scalar_mul(out=meanout[:, :], in0=st1[:, :], scalar1=0.25))
        ctx.chain(eng, eng.reduce_sum(out=st2[:, :], in_=vcols, axis=AXX))
        ctx.chain(eng, eng.tensor_mul(out=sq3, in0=mcols, in1=mcols))
        ctx.chain(eng, eng.reduce_sum(out=st1[:, :], in_=sq3, axis=AXX))
        ctx.chain(eng, eng.tensor_scalar_mul(out=st2[:, :], in0=st2[:, :], scalar1=1.0 / Dm))
        ctx.chain(eng, eng.scalar_tensor_tensor(out=svar[:, :], in0=st1[:, :], scalar=0.25,
                                                in1=st2[:, :], op0=ALU.mult, op1=ALU.add))
        ctx.chain(eng, eng.tensor_mul(out=st1[:, :], in0=meanout[:, :], in1=meanout[:, :]))
        ctx.chain(eng, eng.tensor_sub(out=svar[:, :], in0=svar[:, :], in1=st1[:, :]))
        ctx.inc(eng.tensor_scalar_add(out=svar[:, :], in0=svar[:, :], scalar1=EPS), "sV", 1)
        ctx.mark((tag, "vready"), "sV")
        ctx.wait(eng, (tag, "sqrt"))
        ctx.chain(eng, eng.reciprocal(out=rout[:, :], in_=ssq[:, :]))
        ctx.chain(eng, eng.tensor_mul(out=st1[:, :], in0=rout[:, :], in1=rout[:, :]))
        ctx.chain(eng, eng.tensor_mul(out=st1[:, :], in0=st1[:, :], in1=svar[:, :]))
        ctx.chain(eng, eng.tensor_scalar(out=st1[:, :], in0=st1[:, :], scalar1=-0.5,
                                         scalar2=1.5, op0=ALU.mult, op1=ALU.add))
        ctx.inc(eng.tensor_mul(out=rout[:, :], in0=rout[:, :], in1=st1[:, :]), "sV", 1)

    def emit_vector(eng):
        eng.memset(ones_f[:, :], 1.0)
        eng.memset(ones_bf[:, :], 1.0)
        eng.memset(ones_b1[:, :], 1.0)
        eng.memset(onesc_b[:, :], 1.0)
        eng.memset(id1f[:, :], 1.0)
        ctx.inc(eng.memset(id1b[:, :], 1.0), "sV", 1)
        ctx.mark(("consts",), "sV")
        ctx.wait(eng, ("smalls",))
        ctx.chain(eng, eng.tensor_scalar_mul(out=ba2[:, :], in0=bac[:, :], scalar1=0.5))
        ctx.chain(eng, eng.tensor_scalar_mul(out=b4h[0:1, 1:2], in0=b4h[0:1, 0:1], scalar1=0.5))
        ctx.inc(eng.tensor_scalar_mul(out=gb3h[0:1, 1:2], in0=gb3h[0:1, 0:1], scalar1=0.5), "sV", 1)
        ctx.mark(("bias2",), "sV")

        def comb1(i):
            c4 = slice(4 * i, 4 * i + 4)
            ctx.wait(eng, ("vbn1", i))
            raw = bnraw[:, i * 48:(i + 1) * 48].rearrange("p (s v) -> p s v", v=12)
            mcols = raw[:, :, 1:12:3]
            vcols = raw[:, :, 2:12:3]
            sq3 = stsq[:, 16 * i:16 * (i + 1)].rearrange("p (s v) -> p s v", v=4)
            ctx.chain(eng, eng.reduce_sum(out=st1[:, c4], in_=mcols, axis=AXX))
            ctx.chain(eng, eng.tensor_scalar_mul(out=mean1[:, c4], in0=st1[:, c4],
                                                 scalar1=0.25))
            ctx.chain(eng, eng.reduce_sum(out=st2[:, c4], in_=vcols, axis=AXX))
            ctx.chain(eng, eng.tensor_mul(out=sq3, in0=mcols, in1=mcols))
            ctx.chain(eng, eng.reduce_sum(out=st1[:, c4], in_=sq3, axis=AXX))
            ctx.chain(eng, eng.tensor_scalar_mul(out=st2[:, c4], in0=st2[:, c4],
                                                 scalar1=1.0 / Dm))
            ctx.chain(eng, eng.scalar_tensor_tensor(out=svar[:, c4], in0=st1[:, c4],
                                                    scalar=0.25, in1=st2[:, c4],
                                                    op0=ALU.mult, op1=ALU.add))
            ctx.chain(eng, eng.tensor_mul(out=st1[:, c4], in0=mean1[:, c4],
                                          in1=mean1[:, c4]))
            ctx.chain(eng, eng.tensor_sub(out=svar[:, c4], in0=svar[:, c4],
                                          in1=st1[:, c4]))
            ctx.inc(eng.tensor_scalar_add(out=svar[:, c4], in0=svar[:, c4],
                                          scalar1=EPS), "sV", 1)
            ctx.mark(("s1v", i), "sV")

        def nwtln1(i):
            c4 = slice(4 * i, 4 * i + 4)
            ctx.wait(eng, ("sqp", i))
            ctx.chain(eng, eng.reciprocal(out=r1[:, c4], in_=ssq[:, c4]))
            ctx.chain(eng, eng.tensor_mul(out=st1[:, c4], in0=r1[:, c4], in1=r1[:, c4]))
            ctx.chain(eng, eng.tensor_mul(out=st1[:, c4], in0=st1[:, c4],
                                          in1=svar[:, c4]))
            ctx.chain(eng, eng.tensor_scalar(out=st1[:, c4], in0=st1[:, c4],
                                             scalar1=-0.5, scalar2=1.5,
                                             op0=ALU.mult, op1=ALU.add))
            ctx.chain(eng, eng.tensor_mul(out=r1[:, c4], in0=r1[:, c4], in1=st1[:, c4]))
            if i >= 2:
                ctx.wait(eng, ("lnwh", i - 2))
                ctx.wait(eng, ("lnw", i - 2))
            for s in range(NSUB):
                gs = i * NSUB + s
                sub = bex[:, s * Dm:(s + 1) * Dm]
                last = eng.tensor_scalar(
                    out=xlnbf[i % 2][:, s * Dm:(s + 1) * Dm], in0=sub,
                    scalar1=mean1[:, gs:gs + 1], scalar2=r1[:, gs:gs + 1],
                    op0=ALU.subtract, op1=ALU.mult)
            ctx.inc(last, "sV", 1)
            ctx.mark(("vln", i), "sV")

        def bn1s(i):
            ctx.wait(eng, ("plx", i))
            for s in range(NSUB):
                gs = i * NSUB + s
                sub = bex[:, s * Dm:(s + 1) * Dm]
                eng.bn_stats(out=bnraw[:, gs * 12:gs * 12 + 6], in_=sub[:, 0:512])
                last = eng.bn_stats(out=bnraw[:, gs * 12 + 6:gs * 12 + 12],
                                    in_=sub[:, 512:1024])
            ctx.inc(last, "sV", 1)
            ctx.mark(("vbn1", i), "sV")

        bn1s(0)
        comb1(0)
        nwtln1(0)
        bn1s(1)
        comb1(1)
        nwtln1(1)

        for j in range(NT + 2):
            if 2 <= j:
                i = j - 2
                ctx.wait(eng, ("lx", i))
                for s in range(NSUB):
                    for h in range(2):
                        g = 2 * s + h
                        ctx.wait(eng, ("ty", i, s, h))
                        blk = xb[i % 2][:, s * Dm + h * 512:s * Dm + (h + 1) * 512]
                        ctx.inc(eng.tensor_add(out=blk, in0=blk, in1=pb[4 + g % 2][:, :]),
                                "sV", 1)
                        ctx.mark(("vx1g", i, g), "sV")
                        if h == 1:
                            ctx.mark(("vx1s", i, s), "sV")
                ctx.wait(eng, ("vx1g", i, 7))
                for s in range(NSUB):
                    gs = i * NSUB + s
                    sub = xb[i % 2][:, s * Dm:(s + 1) * Dm]
                    eng.bn_stats(out=bnraw[:, gs * 12:gs * 12 + 6], in_=sub[:, 0:512])
                    last = eng.bn_stats(out=bnraw[:, gs * 12 + 6:gs * 12 + 12],
                                        in_=sub[:, 512:1024])
                ctx.inc(last, "sV", 1)
                ctx.mark(("vbn2", i), "sV")
            if 1 <= j <= NT:
                i = j - 1

                def affine(k):
                    ctx.wait(eng, ("sa", i, k))
                    ctx.chain(eng, eng.tensor_scalar(
                        out=alf[:, k * TT:(k + 1) * TT],
                        in0=alf[:, k * TT:(k + 1) * TT],
                        scalar1=0.5, scalar2=0.5, op0=ALU.mult, op1=ALU.add))

                def scan(k):
                    ctx.wait(eng, ("tb", i, k))
                    ini = 0.0 if i % TPS == 0 else carry[:, k:k + 1]
                    ctx.inc(eng.tensor_tensor_scan(
                        out=hsbf[i % 2][:, k * TT:(k + 1) * TT],
                        data0=alf[:, k * TT:(k + 1) * TT],
                        data1=pb[2 + (k % 2)][:, :],
                        initial=ini, op0=ALU.mult, op1=ALU.add), "sV", 1)
                    ctx.mark(("vsc", i, k), "sV")
                if i >= 2:
                    ctx.wait(eng, ("tyall", i - 2))
                affine(0)
                for k in range(KD):
                    if k + 1 < KD:
                        affine(k + 1)
                    scan(k)
            if 1 <= j <= NT:
                i = j - 1
                ctx.wait(eng, ("vsc", i, KD - 1))
                ctx.inc(eng.engine_nop(), "sV", 1)
                ctx.mark(("vhs", i), "sV")
                if (i + 1) % TPS != 0:
                    ctx.inc(eng.tensor_copy(out=carry[:, :],
                                            in_=hsbf[i % 2][:, TT - 1::TT]), "sV", 1)
            if 2 <= j:
                i = j - 2
                ctx.wait(eng, ("vbn2", i))
                c4 = slice(4 * i, 4 * i + 4)
                raw = bnraw[:, i * 48:(i + 1) * 48].rearrange("p (s v) -> p s v", v=12)
                mcols = raw[:, :, 1:12:3]
                vcols = raw[:, :, 2:12:3]
                sq3 = stsq[:, 16 * i:16 * (i + 1)].rearrange("p (s v) -> p s v", v=4)
                ctx.chain(eng, eng.reduce_sum(out=st1[:, c4], in_=mcols, axis=AXX))
                ctx.chain(eng, eng.tensor_scalar_mul(out=mean2[:, c4], in0=st1[:, c4],
                                                     scalar1=0.25))
                ctx.chain(eng, eng.reduce_sum(out=st2[:, c4], in_=vcols, axis=AXX))
                ctx.chain(eng, eng.tensor_mul(out=sq3, in0=mcols, in1=mcols))
                ctx.chain(eng, eng.reduce_sum(out=st1[:, c4], in_=sq3, axis=AXX))
                ctx.chain(eng, eng.tensor_scalar_mul(out=st2[:, c4], in0=st2[:, c4],
                                                     scalar1=1.0 / Dm))
                ctx.chain(eng, eng.scalar_tensor_tensor(out=svar[:, c4], in0=st1[:, c4],
                                                        scalar=0.25, in1=st2[:, c4],
                                                        op0=ALU.mult, op1=ALU.add))
                ctx.chain(eng, eng.tensor_mul(out=st1[:, c4], in0=mean2[:, c4],
                                              in1=mean2[:, c4]))
                ctx.chain(eng, eng.tensor_sub(out=svar[:, c4], in0=svar[:, c4],
                                              in1=st1[:, c4]))
                ctx.inc(eng.tensor_scalar_add(out=svar[:, c4], in0=svar[:, c4],
                                              scalar1=EPS), "sV", 1)
                ctx.mark(("s2v", i), "sV")
                ctx.wait(eng, ("sqz", i))
                ctx.chain(eng, eng.reciprocal(out=r2t[:, c4], in_=ssq[:, c4]))
                ctx.chain(eng, eng.tensor_mul(out=st1[:, c4], in0=r2t[:, c4],
                                              in1=r2t[:, c4]))
                ctx.chain(eng, eng.tensor_mul(out=st1[:, c4], in0=st1[:, c4],
                                              in1=svar[:, c4]))
                ctx.chain(eng, eng.tensor_scalar(out=st1[:, c4], in0=st1[:, c4],
                                                 scalar1=-0.5, scalar2=1.5,
                                                 op0=ALU.mult, op1=ALU.add))
                ctx.chain(eng, eng.tensor_mul(out=r2t[:, c4], in0=r2t[:, c4],
                                              in1=st1[:, c4]))
                if i >= 2:
                    ctx.wait(eng, ("zw", i - 2))
                    ctx.wait(eng, ("gz", i - 2))
                for s in range(NSUB):
                    gs = i * NSUB + s
                    sub = xb[i % 2][:, s * Dm:(s + 1) * Dm]
                    last = eng.tensor_scalar(
                        out=zsl[i % 2][:, s * Dm:(s + 1) * Dm], in0=sub,
                        scalar1=mean2[:, gs:gs + 1], scalar2=r2t[:, gs:gs + 1],
                        op0=ALU.subtract, op1=ALU.mult)
                ctx.inc(last, "sV", 1)
                ctx.mark(("vz", i), "sV")
            if j + 2 < NT:
                bn1s(j + 2)
                comb1(j + 2)
                nwtln1(j + 2)

            if 2 <= j and (j - 2) % TPS == TPS - 1:
                i = j - 2
                b = seq_of(i)
                ctx.wait(eng, ("gacc", b))
                ctx.chain(eng, eng.tensor_scalar_mul(
                    out=gsb[0:1, b * Dm:b * Dm + 512], in0=pb[6][0:1, :],
                    scalar1=1.0 / T))
                ctx.inc(eng.tensor_scalar_mul(
                    out=gsb[0:1, b * Dm + 512:(b + 1) * Dm], in0=pb[7][0:1, :],
                    scalar1=1.0 / T), "sV", 1)
                ctx.mark(("gsb", b), "sV")
                if b == 1:
                    ctx.wait(eng, ("gfmT", 1))
                    ctx.inc(eng.tensor_copy(out=gfmb[:, 8:16],
                                            in_=pbb6[:, 16:32:2]), "sV", 1)
                    ctx.mark(("gfmb", 1), "sV")
                    ctx.wait(eng, ("tg", 0))
                    ctx.wait(eng, ("tg", 1))
                    ctx.chain(eng, eng.tensor_scalar(out=A2[:, :], in0=tg[:, :],
                                                     scalar1=arv[0:1, 0:1], scalar2=0.7,
                                                     op0=ALU.mult, op1=ALU.mult))
                    ctx.chain(eng, eng.tensor_scalar_add(out=A2[:, :], in0=A2[:, :],
                                                         scalar1=0.1))
                    ctx.inc(eng.tensor_scalar_mul(out=Bc[:, :], in0=arv[:, :],
                                                  scalar1=0.3), "sV", 1)
                    ctx.mark(("a2",), "sV")
            if j == 6:
                ctx.wait(eng, ("gfmT", 0))
                ctx.inc(eng.tensor_copy(out=gfmb[:, 0:8],
                                        in_=pbb6[:, 0:16:2]), "sV", 1)
                ctx.mark(("gfmb", 0), "sV")
        ctx.wait(eng, ("wffn1",))
        ctx.wait(eng, ("wffn2",))
        ctx.wait(eng, ("wffn3",))
        for k in range(KD):
            sgn = -1.0 if (k < 2 or k >= 6) else 1.0
            last = eng.tensor_scalar_mul(out=w1p[k], in0=w1[(k + 4) % KD], scalar1=sgn)
        ctx.inc(last, "sV", 1)
        ctx.mark(("w1p",), "sV")

        def vH0e(i):
            rn = i % 2
            for g in range(KD):
                ctx.wait(eng, ("tH0", i, g))
                ctx.inc(eng.tensor_copy(out=H0[rn][:, g * TT:(g + 1) * TT],
                                        in_=pb[g % 2][:, :]), "sV", 1)
                ctx.mark(("eH0", i, g), "sV")

        def vFinal(i, subs):
            o = cco[i % 2]
            a3, b3v = o[:, 12:16], o[:, 16:20]
            tmp = o[:, 20:24]
            if subs[0] == 0:
                ctx.wait(eng, ("b2x", i))
            for s in subs:
                gs = i * NSUB + s
                a3c, b3c_ = a3[:, s:s + 1], b3v[:, s:s + 1]
                r2c = r2t[:, gs:gs + 1]
                m2c = mean2[:, gs:gs + 1]
                P1, PB, PBn, CP, CMn = (cvf[:, 0:1], cvf[:, 1:2], cvf[:, 2:3],
                                        cvf[:, 3:4], cvf[:, 4:5])
                ctx.chain(eng, eng.tensor_mul(out=P1, in0=a3c, in1=r2c))
                ctx.chain(eng, eng.tensor_scalar_add(out=P1, in0=P1, scalar1=1.0))
                ctx.chain(eng, eng.tensor_mul(out=PB, in0=b3c_, in1=r2c))
                ctx.chain(eng, eng.tensor_scalar_mul(out=PBn, in0=PB, scalar1=-1.0))
                ctx.chain(eng, eng.tensor_mul(out=tmp[:, s:s + 1], in0=m2c, in1=r2c))
                ctx.chain(eng, eng.tensor_add(out=CP, in0=a3c, in1=b3c_))
                ctx.chain(eng, eng.tensor_mul(out=CP, in0=CP, in1=tmp[:, s:s + 1]))
                ctx.chain(eng, eng.tensor_scalar_mul(out=CP, in0=CP, scalar1=-1.0))
                ctx.chain(eng, eng.tensor_sub(out=CMn, in0=b3c_, in1=a3c))
                ctx.chain(eng, eng.tensor_mul(out=CMn, in0=CMn, in1=tmp[:, s:s + 1]))
                x1s = x1b[i % 2][:, s * Dm:(s + 1) * Dm]
                for (qa, qb) in ((0, 2), (1, 3)):
                    blka = x1s[:, qa * 256:(qa + 1) * 256]
                    blkb = x1s[:, qb * 256:(qb + 1) * 256]
                    ta = fsc[:, 0:256]
                    tb_ = fsc[:, 256:512]
                    if qa == 0:
                        ctx.chain(eng, eng.tensor_scalar(out=ta, in0=blkb, scalar1=PB,
                                                         scalar2=CP, op0=ALU.mult, op1=ALU.add))
                        ctx.chain(eng, eng.tensor_scalar(out=tb_, in0=blka, scalar1=PBn,
                                                         scalar2=CMn, op0=ALU.mult, op1=ALU.add))
                    else:
                        ctx.chain(eng, eng.tensor_scalar(out=ta, in0=blkb, scalar1=PBn,
                                                         scalar2=CMn, op0=ALU.mult, op1=ALU.add))
                        ctx.chain(eng, eng.tensor_scalar(out=tb_, in0=blka, scalar1=PB,
                                                         scalar2=CP, op0=ALU.mult, op1=ALU.add))
                    ctx.chain(eng, eng.scalar_tensor_tensor(out=blka, in0=blka, scalar=P1,
                                                            in1=ta, op0=ALU.mult, op1=ALU.add))
                    ctx.chain(eng, eng.scalar_tensor_tensor(out=blkb, in0=blkb, scalar=P1,
                                                            in1=tb_, op0=ALU.mult, op1=ALU.add))
            if subs[-1] == NSUB - 1:
                ctx.inc(eng.engine_nop(), "sV", 1)
                ctx.mark(("vfin", i), "sV")

        vH0e(0)
        for i in range(NT):
            r = i % 2
            b = seq_of(i)
            if i <= 1:
                ctx.wait(eng, ("zw", NT - 1))
                ctx.wait(eng, ("zw", NT - 2))
                ctx.wait(eng, ("gz", NT - 1))
            for cyc in (1, 2, 3):
                if cyc == 2:
                    ctx.wait(eng, ("cbe", i, 1))
                    ctx.wait(eng, ("eH0", i, KD - 1))
                    if i >= 2:
                        ctx.wait(eng, ("gel1", i - 2, 3, KD - 1))
                    for k in range(KD):
                        last = eng.tensor_mul(out=Gcur[r][:, k * TT:(k + 1) * TT],
                                              in0=C1b[r][:, :],
                                              in1=H0[r][:, k * TT:(k + 1) * TT])
                    ctx.chain(eng, last)
                    for k in range(KD):
                        ctx.inc(eng.tensor_add(out=Gcur[r][:, k * TT:(k + 1) * TT],
                                               in0=Gcur[r][:, k * TT:(k + 1) * TT],
                                               in1=G0[r][:, k * TT:(k + 1) * TT]),
                                "sV", 1)
                        ctx.mark(("gck", i, 2, k), "sV")
                    ctx.mark(("gc", i, 2), "sV")
                    if i >= 1:
                        vFinal(i - 1, (2, 3))
                if cyc == 3:
                    c1r = cf[r][0:1, 0:512]
                    c2r = cf[r][0:1, 512:1024]
                    p2r = cf[r][0:1, 1536:2048]
                    q2r = cf[r][0:1, 2048:2560]
                    ctx.wait(eng, ("cfm", i, 2))
                    ctx.chain(eng, eng.tensor_mul(out=p2r, in0=c1r, in1=c2r))
                    ctx.chain(eng, eng.tensor_scalar(out=p2r, in0=p2r, scalar1=-1.0,
                                                     scalar2=1.0, op0=ALU.mult, op1=ALU.add))
                    ctx.inc(eng.tensor_add(out=q2r, in0=c1r, in1=c2r), "sV", 1)
                    ctx.mark(("p2q2", i), "sV")
                    ctx.wait(eng, ("cbe", i, 3))
                    ctx.wait(eng, ("th2", i, 2, 3))
                    for k in range(KD):
                        eng.tensor_mul(out=Gcur[r][:, k * TT:(k + 1) * TT],
                                       in0=P2b[r][:, :], in1=G0[r][:, k * TT:(k + 1) * TT])
                    for k in range(KD):
                        last = eng.tensor_mul(out=h1t[r][:, k * TT:(k + 1) * TT],
                                              in0=Q2b[r][:, :],
                                              in1=H0[r][:, k * TT:(k + 1) * TT])
                    ctx.chain(eng, last)
                    for k in range(KD):
                        ctx.inc(eng.tensor_add(out=Gcur[r][:, k * TT:(k + 1) * TT],
                                               in0=Gcur[r][:, k * TT:(k + 1) * TT],
                                               in1=h1t[r][:, k * TT:(k + 1) * TT]),
                                "sV", 1)
                        ctx.mark(("gck", i, 3, k), "sV")
                    ctx.mark(("gc", i, 3), "sV")
                    if i + 1 < NT:
                        vH0e(i + 1)
                ctx.wait(eng, ("t4", i, cyc))
                ctx.wait(eng, ("a2",))
                if cyc == 1 and i >= 2:
                    ctx.wait(eng, ("ctT", i - 2))
                    ctx.wait(eng, ("tcb", i - 2, 3))
                ccol = cf[r][0:1, (cyc - 1) * 512:cyc * 512]
                ctx.inc(eng.tensor_scalar(
                    out=ccol, in0=t4b[r][0:1, :], scalar1=Bc[0:1, 0:1],
                    scalar2=A2[0:1, b:b + 1], op0=ALU.mult, op1=ALU.add), "sV", 1)
                ctx.mark(("cfm", i, cyc), "sV")

            ctx.wait(eng, ("ctT", i))
            ctx.inc(eng.tensor_copy(out=ctv[r][:, :], in_=pbb6[:, 0:24:2]), "sV", 1)
            ctx.mark(("ctv", i), "sV")
            ctx.wait(eng, ("ctv", i))
            cc1, cc2, cc3 = ctv[r][:, 0:4], ctv[r][:, 4:8], ctv[r][:, 8:12]
            o = cco[r]
            q12, q13, q23 = o[:, 0:4], o[:, 4:8], o[:, 8:12]
            a3, b3v = o[:, 12:16], o[:, 16:20]
            tmp = o[:, 20:24]
            ctx.chain(eng, eng.tensor_mul(out=q12, in0=cc1, in1=cc2))
            eng.tensor_mul(out=q13, in0=cc1, in1=cc3)
            ctx.chain(eng, eng.tensor_mul(out=q23, in0=cc2, in1=cc3))
            ctx.chain(eng, eng.tensor_add(out=a3, in0=q12, in1=q13))
            ctx.chain(eng, eng.tensor_add(out=a3, in0=a3, in1=q23))
            ctx.chain(eng, eng.tensor_scalar(out=a3, in0=a3, scalar1=-1.0, scalar2=1.0,
                                             op0=ALU.mult, op1=ALU.add))
            ctx.chain(eng, eng.tensor_mul(out=tmp, in0=q12, in1=cc3))
            ctx.chain(eng, eng.tensor_add(out=b3v, in0=cc1, in1=cc2))
            ctx.chain(eng, eng.tensor_add(out=b3v, in0=b3v, in1=cc3))
            ctx.chain(eng, eng.tensor_sub(out=b3v, in0=b3v, in1=tmp))
            vFinal(i, (0, 1))
        vFinal(NT - 1, (2, 3))

    def emit_scalar(eng):
        def sqp(i):
            ctx.wait(eng, ("s1v", i))
            ctx.inc(eng.activation(out=ssq[:, 4 * i:4 * i + 4],
                                   in_=svar[:, 4 * i:4 * i + 4], func=AF.Sqrt,
                                   bias=0.0, scale=1.0), "sS", 1)
            ctx.mark(("sqp", i), "sS")

        def sqz(i):
            ctx.wait(eng, ("s2v", i))
            ctx.inc(eng.activation(out=ssq[:, 4 * i:4 * i + 4],
                                   in_=svar[:, 4 * i:4 * i + 4], func=AF.Sqrt,
                                   bias=0.0, scale=1.0), "sS", 1)
            ctx.mark(("sqz", i), "sS")

        def gacts(b):
            for m in range(4):
                ctx.wait(eng, ("gc1", b, m))
                ctx.inc(eng.activation(out=hg1[:, b * 4 + m:b * 4 + m + 1],
                                       in_=pb[6][:, 64 + b * 4 + m:65 + b * 4 + m],
                                       func=AF.Gelu,
                                       bias=gb1c[:, m:m + 1], scale=1.0), "sS", 1)
            ctx.mark(("hg1", b), "sS")
            for m in range(2):
                ctx.wait(eng, ("gc2", b, m))
                ctx.inc(eng.activation(out=hg2[:, b * 2 + m:b * 2 + m + 1],
                                       in_=pb[6][:, 80 + b * 2 + m:81 + b * 2 + m],
                                       func=AF.Gelu, bias=gb2c[:, m:m + 1], scale=1.0),
                        "sS", 1)
            ctx.mark(("hg2", b), "sS")
            ctx.wait(eng, ("gc3", b))
            ctx.inc(eng.activation(out=tg[0:1, b:b + 1], in_=pb[6][0:1, 96 + b:97 + b],
                                   func=AF.Tanh, bias=gb3h[0:1, 1:2], scale=0.5), "sS", 1)
            ctx.mark(("tg", b), "sS")

        def spillx(i):
            sl = "sLN%d" % (i % 2)
            sb = "sLB%d" % (i % 2)
            src3 = xlnbf[i % 2].rearrange("p (j d) -> p j d", d=Dm)
            ctx.wait(eng, ("vln", i))
            ctx.epoch(eng, sl)
            ctx.inc(eng.dma_start(
                out=xln_scr[tok0(i):tok0(i) + TT, 0:512].rearrange(
                    "(j p) d -> p j d", p=128),
                in_=src3[:, :, 0:512]), sl, 16)
            ctx.mark(("lnwh", i), sl)
            ctx.epoch(eng, sb)
            ctx.inc(eng.dma_start(
                out=xln_scr[tok0(i):tok0(i) + TT, 512:1024].rearrange(
                    "(j p) d -> p j d", p=128),
                in_=src3[:, :, 512:1024]), sb, 16)
            ctx.mark(("lnw", i), sb)

        def spillz(i):
            sb = "sLB%d" % (i % 2)
            ctx.wait(eng, ("vz", i))
            ctx.epoch(eng, sb)
            ctx.inc(eng.dma_start(
                out=z_scr[tok0(i):tok0(i) + TT, :].rearrange(
                    "(j p) d -> p j d", p=128),
                in_=zsl[i % 2].rearrange("p (j d) -> p j d", d=Dm)), sb, 16)
            ctx.mark(("zw", i), sb)

        sqp(0)
        spillx(0)
        sqp(1)
        spillx(1)
        ctx.wait(eng, ("bias2",))

        for i in range(NT):
            if i + 2 < NT:
                sqp(i + 2)
            for m in range(KD):
                ctx.wait(eng, ("ta", i, m))
                if i >= 1:
                    ctx.wait(eng, ("vsc", i - 1, m))
                ctx.inc(eng.activation(out=alf[:, m * TT:(m + 1) * TT],
                                       in_=pb[m % 2][:, :], func=AF.Tanh,
                                       bias=ba2[:, m:m + 1], scale=0.5), "sS", 1)
                ctx.mark(("sa", i, m), "sS")
            if i >= 1:
                sqz(i - 1)
                spillz(i - 1)
            if i + 2 < NT:
                spillx(i + 2)
            if i == 6:
                gacts(0)
        sqz(NT - 1)
        spillz(NT - 1)
        gacts(1)
        def sG0e(i):
            r = i % 2
            if i == 1:
                ctx.wait(eng, ("zw", NT - 1))
                ctx.wait(eng, ("zw", NT - 2))
                ctx.wait(eng, ("gz", NT - 1))
            if i >= 2:
                ctx.wait(eng, ("gc", i - 2, 3))
                ctx.wait(eng, ("gel1", i - 2, 3, KD - 1))
            for g in range(KD):
                ctx.wait(eng, ("tG0", i, g))
                ctx.inc(eng.copy(out=G0[r][:, g * TT:(g + 1) * TT], in_=pb[g % 2][:, :]),
                        "sS", 1)
                ctx.mark(("eG0", i, g), "sS")

        def sCyc(i, cyc):
            r = i % 2
            if cyc == 1 and i >= 2:
                ctx.wait(eng, ("th2", i - 2, 3, 3))
                ctx.wait(eng, ("th3", i - 2, 3, 1))
                ctx.wait(eng, ("ttf", i - 2, 3))
                ctx.wait(eng, ("cfm", i - 2, 3))
            if cyc == 3:
                ctx.wait(eng, ("tcb", i, 3))
                eng.copy(out=P2b[r][:, :], in_=pb[3][:, :])
                ctx.inc(eng.copy(out=Q2b[r][:, :], in_=pb[5][:, :]), "sS", 1)
                ctx.mark(("cbe", i, 3), "sS")
            src_ = G0[r] if cyc == 1 else Gcur[r]
            if i <= 1 and cyc == 1:
                ctx.wait(eng, ("zw", NT - 1))
                ctx.wait(eng, ("zw", NT - 2))
                ctx.wait(eng, ("gz", NT - 1))
            if cyc == 1:
                ctx.wait(eng, ("eG0", i, KD - 1))
            for k in range(KD):
                if cyc > 1:
                    ctx.wait(eng, ("gck", i, cyc, k))
                ctx.inc(eng.activation(out=h1t[r][:, k * TT:(k + 1) * TT],
                                       in_=src_[:, k * TT:(k + 1) * TT], func=AF.Gelu,
                                       bias=b1c[:, k:k + 1], scale=1.0), "sS", 1)
                ctx.mark(("gel1", i, cyc, k), "sS")
            for m in range(4):
                ctx.wait(eng, ("th2", i, cyc, m))
                ctx.inc(eng.activation(out=h2t[r][:, m * TT:(m + 1) * TT],
                                       in_=pb[m % 2][:, :], func=AF.Gelu,
                                       bias=b2c[:, m:m + 1], scale=1.0), "sS", 1)
                ctx.mark(("gel2", i, cyc, m), "sS")
            for m in range(2):
                ctx.wait(eng, ("th3", i, cyc, m))
                ctx.inc(eng.activation(out=h3t[r][:, m * TT:(m + 1) * TT],
                                       in_=pb[m % 2][:, :], func=AF.Gelu,
                                       bias=b3c[:, m:m + 1], scale=1.0), "sS", 1)
                ctx.mark(("gel3", i, cyc, m), "sS")
            ctx.wait(eng, ("ttf", i, cyc))
            if cyc > 1:
                ctx.wait(eng, ("cfm", i, cyc - 1))
            elif i > 0:
                ctx.wait(eng, ("cfm", i - 1, 3))
            ctx.inc(eng.activation(out=t4b[r][0:1, :], in_=pb[2][0:1, :],
                                   func=AF.Tanh, bias=b4h[0:1, 1:2], scale=0.5), "sS", 1)
            ctx.mark(("t4", i, cyc), "sS")
            if cyc == 1:
                ctx.wait(eng, ("tcb", i, 1))
                ctx.inc(eng.copy(out=C1b[r][:, :], in_=pb[3][:, :]), "sS", 1)
                ctx.mark(("cbe", i, 1), "sS")

        for i in range(NT):
            if i >= 1:
                sCyc(i - 1, 1)
            sG0e(i)
            if i >= 1:
                sCyc(i - 1, 2)
                sCyc(i - 1, 3)
        sCyc(NT - 1, 1)
        sCyc(NT - 1, 2)
        sCyc(NT - 1, 3)
    def emit_tensor(eng):
        ctx.wait(eng, ("wattn1",))

        def zmean(zi):
            b = seq_of(zi)
            first = (zi % TPS == 0)
            last_t = (zi % TPS == TPS - 1)
            if zi == TPS:
                ctx.wait(eng, ("tg", 0))
            ctx.wait(eng, ("vz", zi))
            for s in range(NSUB):
                for h in range(2):
                    mm = eng.matmul(pb[6 + h][0:1, :], onesc_b[:, :],
                                    zsl[zi % 2][:, s * Dm + h * 512:s * Dm + (h + 1) * 512],
                                    start=(first and s == 0),
                                    stop=(last_t and s == NSUB - 1),
                                    skip_group_check=True)
            ctx.inc(mm, "sT", 1)
            ctx.mark(("gz", zi), "sT")
            if last_t:
                ctx.mark(("gacc", b), "sT")

        def gfmT(b):
            ctx.wait(eng, ("gsb", b))
            for k in range(KD):
                c_ = 2 * (b * 8 + k)
                mm = eng.matmul(pbb6[:, c_:c_ + 1],
                                gsb[0:1, b * Dm + k * 128:b * Dm + (k + 1) * 128],
                                id1b[0:1, 0:1], start=(k == 0), stop=(k == KD - 1),
                                is_transpose=True, skip_group_check=True)
            ctx.inc(mm, "sT", 1)
            ctx.mark(("gfmT", b), "sT")

        def gcpart(b):
            ctx.wait(eng, ("wffn3",))
            ctx.wait(eng, ("gfmb", b))
            for m in range(4):
                for k in range(KD):
                    mm = eng.matmul(pb[6][:, 64 + b * 4 + m:65 + b * 4 + m],
                                    g1[k][:, m * 128:(m + 1) * 128],
                                    gfmb[:, b * 8 + k:b * 8 + k + 1],
                                    start=False, stop=(k == KD - 1),
                                    skip_group_check=True)
                ctx.inc(mm, "sT", 1)
                ctx.mark(("gc1", b, m), "sT")
            ctx.wait(eng, ("hg1", b))
            for m in range(2):
                for k in range(4):
                    mm = eng.matmul(pb[6][:, 80 + b * 2 + m:81 + b * 2 + m],
                                    g2[k][:, m * 128:(m + 1) * 128],
                                    hg1[:, b * 4 + k:b * 4 + k + 1],
                                    start=False, stop=(k == 3),
                                    skip_group_check=True)
                ctx.inc(mm, "sT", 1)
                ctx.mark(("gc2", b, m), "sT")
            ctx.wait(eng, ("hg2", b))
            for k in range(2):
                mm = eng.matmul(pb[6][0:1, 96 + b:97 + b], g3[k][0:128, 0:1],
                                hg2[:, b * 2 + k:b * 2 + k + 1],
                                start=False, stop=(k == 1), skip_group_check=True)
            ctx.inc(mm, "sT", 1)
            ctx.mark(("gc3", b), "sT")

        def yblk(yi):
            ctx.wait(eng, ("vhs", yi))
            if yi == 0:
                ctx.wait(eng, ("dstrl",))
                ctx.wait(eng, ("wattn3",))
            for s in range(NSUB):
                for h in range(2):
                    g = 2 * s + h
                    if yi > 0 or g >= 2:
                        pi, pg = (yi, g - 2) if g >= 2 else (yi - 1, g + 6)
                        ctx.wait(eng, ("vx1g", pi, pg))
                    for k in range(KD):
                        eng.matmul(
                            pb[4 + g % 2][:, :],
                            hsbf[yi % 2][:, k * TT + s * 128:k * TT + (s + 1) * 128],
                            cT[k][:, h * 512:(h + 1) * 512],
                            start=(k == 0), stop=False)
                    for kr in range(4):
                        kdm = 4 * h + kr
                        mm = eng.matmul(
                            pb[4 + g % 2][:, :],
                            xfm[yi % 2][:, kdm * TT + s * 128:kdm * TT + (s + 1) * 128],
                            dstrb[:, (h * 4 + kr) * 512:(h * 4 + kr + 1) * 512],
                            start=False, stop=(kr == 3))
                    ctx.inc(mm, "sT", 1)
                    ctx.mark(("ty", yi, s, h), "sT")
            ctx.mark(("tyall", yi), "sT")

        for i in range(NT):
            ctx.wait(eng, ("tr", i))
            for m in range(KD):
                if i > 0 or m >= 2:
                    pi, pm = (i, m - 2) if m >= 2 else (i - 1, m + 6)
                    ctx.wait(eng, ("sa", pi, pm))
                for k in range(KD):
                    mm = eng.matmul(pb[m % 2][:, :], waT[k][:, m * 128:(m + 1) * 128],
                                    xfm[i % 2][:, k * TT:(k + 1) * TT],
                                    start=(k == 0), stop=(k == KD - 1))
                ctx.inc(mm, "sT", 1)
                ctx.mark(("ta", i, m), "sT")
            if 2 <= i <= 5:
                zmean(i - 2)
            if i == 7:
                zmean(4)
                zmean(5)
            if i == 6:
                gfmT(0)
            if i > 0:
                yblk(i - 1)
            if i == 0:
                ctx.wait(eng, ("wattn2",))
                ctx.wait(eng, ("smalls",))
                ctx.wait(eng, ("consts",))
            for m in range(KD):
                if i > 0 or m >= 2:
                    pi, pm = (i, m - 2) if m >= 2 else (i - 1, m + 6)
                    ctx.wait(eng, ("vsc", pi, pm))
                for k in range(KD):
                    eng.matmul(pb[2 + m % 2][:, :], wbT[k][:, m * 128:(m + 1) * 128],
                               xfm[i % 2][:, k * TT:(k + 1) * TT],
                               start=(k == 0), stop=False)
                mm = eng.matmul(pb[2 + m % 2][:, :], bbrow[0:1, m * 128:(m + 1) * 128],
                                ones_b1[0:1, :], start=False, stop=True)
                ctx.inc(mm, "sT", 1)
                ctx.mark(("tb", i, m), "sT")
            ctx.mark(("tmma", i), "sT")
            if i == 6:
                gcpart(0)
            if i == NT - 1:
                yblk(i)
                zmean(6)
                zmean(7)
                gfmT(1)
                gcpart(1)

        bankq = [None, None]

        def bigmm(consumer_mark, emitfn):
            j = bigmm.idx
            bigmm.idx += 1
            bk = j % 2
            if bankq[bk] is not None:
                ctx.wait(eng, bankq[bk])
            emitfn(pb[bk])
            bankq[bk] = consumer_mark
        bigmm.idx = 0

        ctx.wait(eng, ("w1p",))
        ctx.wait(eng, ("wffn1",))
        ctx.wait(eng, ("wffn2",))
        def tG0(i):
            r = i % 2
            ctx.wait(eng, ("ztr", i))
            for g in range(KD):
                def f(bank, g=g, i=i, r=r):
                    for k in range(KD):
                        mm = eng.matmul(bank[:, :], w1[k][:, g * 128:(g + 1) * 128],
                                        zfm[r][:, k * TT:(k + 1) * TT],
                                        start=(k == 0), stop=(k == KD - 1))
                    ctx.inc(mm, "sT", 1)
                    ctx.mark(("tG0", i, g), "sT")
                bigmm(("eG0", i, g), f)

        def tH0(i):
            r = i % 2
            for g in range(KD):
                def f(bank, g=g, i=i, r=r):
                    for k in range(KD):
                        mm = eng.matmul(bank[:, :], w1p[k][:, g * 128:(g + 1) * 128],
                                        zfm[r][:, k * TT:(k + 1) * TT],
                                        start=(k == 0), stop=(k == KD - 1))
                    ctx.inc(mm, "sT", 1)
                    ctx.mark(("tH0", i, g), "sT")
                bigmm(("eH0", i, g), f)
            ctx.mark(("tg0h0", i), "sT")

        def tCyc(i, cyc):
            r = i % 2
            if True:
                for m in range(4):
                    def f(bank, m=m, i=i, r=r, cyc=cyc):
                        for k in range(KD):
                            if m == 0:
                                ctx.wait(eng, ("gel1", i, cyc, k))
                            mm = eng.matmul(bank[:, :], w2[k][:, m * 128:(m + 1) * 128],
                                            h1t[r][:, k * TT:(k + 1) * TT],
                                            start=(k == 0), stop=(k == KD - 1))
                        ctx.inc(mm, "sT", 1)
                        ctx.mark(("th2", i, cyc, m), "sT")
                    bigmm(("gel2", i, cyc, m), f)
                for m in range(2):
                    def f(bank, m=m, i=i, r=r, cyc=cyc):
                        for k in range(4):
                            if m == 0:
                                ctx.wait(eng, ("gel2", i, cyc, k))
                            mm = eng.matmul(bank[:, :], w3[k][:, m * 128:(m + 1) * 128],
                                            h2t[r][:, k * TT:(k + 1) * TT],
                                            start=(k == 0), stop=(k == 3))
                        ctx.inc(mm, "sT", 1)
                        ctx.mark(("th3", i, cyc, m), "sT")
                    bigmm(("gel3", i, cyc, m), f)
                if i == 0 and cyc == 1:
                    ctx.wait(eng, ("tg", 0))
                    ctx.wait(eng, ("tg", 1))
                else:
                    ctx.wait(eng, ("t4", i, cyc - 1) if cyc > 1 else ("t4", i - 1, 3))
                for k in range(2):
                    ctx.wait(eng, ("gel3", i, cyc, k))
                    mm = eng.matmul(pb[2][0:1, :], w4[k][0:128, 0:1],
                                    h3t[r][:, k * TT:(k + 1) * TT],
                                    start=(k == 0), stop=(k == 1))
                ctx.inc(mm, "sT", 1)
                ctx.mark(("ttf", i, cyc), "sT")
                if cyc == 1:
                    ctx.wait(eng, ("cfm", i, 1))
                    if i > 0:
                        ctx.wait(eng, ("cbe", i - 1, 3))
                    ctx.inc(eng.matmul(pb[3][:, :], ones_bf[:, :], cf[r][0:1, 0:512],
                                       start=True, stop=True), "sT", 1)
                    ctx.mark(("tcb", i, 1), "sT")
                ctx.wait(eng, ("cfm", i, cyc))
                if cyc == 1 and i >= 1:
                    ctx.wait(eng, ("ctv", i - 1))
                for s in range(NSUB):
                    c_ = 2 * (4 * (cyc - 1) + s)
                    mm = eng.matmul(
                        pbb6[:, c_:c_ + 1],
                        cf[r][0:1, (cyc - 1) * 512 + s * 128:(cyc - 1) * 512 + (s + 1) * 128],
                        id1b[0:1, 0:1],
                        start=(cyc == 1 and s == 0), stop=(cyc == 3 and s == 3),
                        is_transpose=True, skip_group_check=True)
                if cyc == 3:
                    ctx.inc(mm, "sT", 1)
                    ctx.mark(("ctT", i), "sT")
                if cyc == 2:
                    ctx.wait(eng, ("p2q2", i))
                    ctx.wait(eng, ("cbe", i, 1))
                    if i > 0:
                        ctx.wait(eng, ("cbe", i - 1, 3))
                    eng.matmul(pb[3][:, :], ones_bf[:, :], cf[r][0:1, 1536:2048],
                               start=True, stop=True)
                    ctx.inc(eng.matmul(pb[5][:, :], ones_bf[:, :], cf[r][0:1, 2048:2560],
                                       start=True, stop=True), "sT", 1)
                    ctx.mark(("tcb", i, 3), "sT")


        for i in range(NT):
            if i >= 1:
                tCyc(i - 1, 1)
            tG0(i)
            if i >= 1:
                tCyc(i - 1, 2)
            tH0(i)
            if i >= 1:
                tCyc(i - 1, 3)
        tCyc(NT - 1, 1)
        tCyc(NT - 1, 2)
        tCyc(NT - 1, 3)
    ctx.reset(False)
    dummy = _DummyEng()
    emit_gpsimd(dummy)
    emit_sync(dummy)
    emit_vector(dummy)
    emit_scalar(dummy)
    emit_tensor(dummy)

    import contextlib
    with contextlib.ExitStack() as stack:
        for s in Ctx.SEMS:
            ctx.handles[s] = stack.enter_context(nc.semaphore(s))
        block = stack.enter_context(nc.Block())
        ctx.reset(True)
        block.gpsimd(lambda e: emit_gpsimd(e))
        block.sync(lambda e: emit_sync(e))
        block.vector(lambda e: emit_vector(e))
        block.scalar(lambda e: emit_scalar(e))
        block.tensor(lambda e: emit_tensor(e))
    return nc


def kernel(**inputs):
    u = np.ascontiguousarray(inputs["L0_attn_u"])

    def chunked(v, nk):
        return np.ascontiguousarray(np.asarray(v).reshape(nk, 128).T)

    def dstrips(D_):
        ds = np.zeros((8, 128, 512), np.float32)
        idx = np.arange(128)
        for h in range(2):
            for kr in range(4):
                ds[h * 4 + kr, idx, kr * 128 + idx] = D_[(4 * h + kr) * 128:
                                                         (4 * h + kr + 1) * 128]
        return ds

    shared = {
        "waT": np.ascontiguousarray(np.asarray(inputs["L0_attn_W_alpha_w"]).T),
        "wbT": np.ascontiguousarray(np.asarray(inputs["L0_attn_W_B_w"]).T),
        "cT": np.ascontiguousarray(np.asarray(inputs["L0_attn_C_w"]).T),
        "w1": np.ascontiguousarray(inputs["L0_ffn_cn_w1"]),
        "w2": np.ascontiguousarray(inputs["L0_ffn_cn_w2"]),
        "w3": np.ascontiguousarray(inputs["L0_ffn_cn_w3"]),
        "w4": np.ascontiguousarray(inputs["L0_ffn_cn_w4"]),
        "g1": np.ascontiguousarray(inputs["L0_ffn_gc_w1"]),
        "g2": np.ascontiguousarray(inputs["L0_ffn_gc_w2"]),
        "g3": np.ascontiguousarray(inputs["L0_ffn_gc_w3"]),
        "ba": chunked(inputs["L0_attn_W_alpha_b"], 8),
        "bbrow_d": np.ascontiguousarray(np.asarray(inputs["L0_attn_W_B_b"]).reshape(1, Dm)),
        "b1": chunked(inputs["L0_ffn_cn_b1"], 8),
        "b2": chunked(inputs["L0_ffn_cn_b2"], 4),
        "b3": chunked(inputs["L0_ffn_cn_b3"], 2),
        "b4": np.ascontiguousarray(np.asarray(inputs["L0_ffn_cn_b4"]).reshape(1, 1)),
        "gb1": chunked(inputs["L0_ffn_gc_b1"], 4),
        "gb2": chunked(inputs["L0_ffn_gc_b2"], 2),
        "gb3": np.ascontiguousarray(np.asarray(inputs["L0_ffn_gc_b3"]).reshape(1, 1)),
        "ar": np.ascontiguousarray(np.asarray(inputs["L0_ffn_adaptive_range"]).reshape(1, 1)),
        "dstr": dstrips(np.asarray(inputs["L0_attn_D"]).reshape(Dm)),
    }
    nc = build_program()
    in_maps = []
    for c in range(NCORES):
        m = dict(shared)
        m["u"] = np.ascontiguousarray(u[SEQ * c:SEQ * (c + 1)])
        in_maps.append(m)
    res = run_bass_kernel_spmd(nc, in_maps, core_ids=list(range(NCORES)))
    return np.concatenate([np.asarray(r["out"]) for r in res.results], axis=0)


if __name__ == "__main__":
    build_program()
    print("built OK")
